# version 01 (final)
# baseline (speedup 1.0000x reference)
import math
import numpy as np
import concourse.bass as bass
import concourse.mybir as mybir
from concourse.bass_utils import run_bass_kernel_spmd

F32 = mybir.dt.float32
BF16 = mybir.dt.bfloat16
AF = mybir.ActivationFunctionType
ALU = mybir.AluOpType
AX = mybir.AxisListType

NCORE = 8
D = 1024
TSH = 2048
NT = TSH // 128
DC = D // 128
T_FULL = 8192
NHEAD = 16
HD = 64
EPS = 1e-5
DEBUG_CW = False


class Prog:
    def __init__(self, nc):
        import contextlib
        self.nc = nc
        self.q = {e: [] for e in ("sync", "tensor", "vector", "scalar", "gpsimd")}
        self.stack = contextlib.ExitStack()
        self.sems = {}
        self.cnt = {}
    def mm(self, e, out, lhsT, rhs, n_bytes, **kw):
        return e.matmul(out, lhsT, rhs, **kw)

    def sig(self, e, name, by=1):
        e.drain()
        e.nop().then_inc(self.sems[name], by)

    def sem(self, name):
        if name not in self.sems:
            self.sems[name] = self.stack.enter_context(self.nc.semaphore(name))
            self.cnt[name] = 0
        return self.sems[name]

    def inc(self, name, by=1):
        s = self.sem(name)
        self.cnt[name] += by
        return s, self.cnt[name]

    def cur(self, name):
        self.sem(name)
        return self.cnt[name]

    def sb(self, name, shape, dt):
        return self.stack.enter_context(self.nc.sbuf_tensor(name, shape, dt))

    def ps(self, name, shape, dt):
        return self.stack.enter_context(self.nc.psum_tensor(name, shape, dt))

    def on(self, e):
        def deco(fn):
            self.q[e].append(fn)
            return None
        return deco

    def finish(self):
        qs = self.q
        with self.nc.Block() as block:
            block.sync(lambda e: [f(e) for f in qs["sync"]])
            block.tensor(lambda e: [f(e) for f in qs["tensor"]])
            block.vector(lambda e: [f(e) for f in qs["vector"]])
            block.scalar(lambda e: [f(e) for f in qs["scalar"]])
            block.gpsimd(lambda e: [f(e) for f in qs["gpsimd"]])
        self.stack.close()
        return self.nc


def new_nc():
    return bass.Bass("TRN2", target_bir_lowering=False, debug=False,
                     detect_race_conditions=False)


def emit_consts(p):
    K = p.sb("konst", [128, 8], F32)
    ones = p.sb("ones", [1, 512], BF16)
    p.sem("s_k")
    vals = {"invD": 1.0 / D, "neginvD": -1.0 / D, "eps": float(EPS),
            "negh": -0.5, "one": 1.0, "eighth": 0.125, "neginvHD": -1.0 / 64}
    idx = {k: i for i, k in enumerate(vals)}
    @p.on("gpsimd")
    def _(e):
        for k, i in idx.items():
            e.memset(K[:, i:i + 1], vals[k])
        e.memset(ones[:, :], 1.0)
        p.sig(e, "s_k")
    p.cnt["s_k"] = 1
    cols = {k: K[:, i:i + 1] for k, i in idx.items()}
    cols["ones"] = ones
    return cols


def emit_ln_transpose(p, X, XT, XLNROT, ident, sc, kc, tag, add_half_to_stream=False,
                      wait=None, wait_consts=None, keep_tokmaj=None, nt=NT):
    xsc = sc["xln_f32"]
    st = sc["stats"]
    pst = p.ps("pst_" + tag, [128, 2, 1024], BF16)
    n = 0
    for t in range(nt):
        r = 0
        _, v_sq = p.inc("sq_" + tag)
        @p.on("scalar")
        def _(e, t=t, r=r, v=wait, tag=tag):
            if t == 0:
                if v is not None:
                    e.wait_ge(v[0], v[1])
                e.wait_ge(p.sems["s_k"], 1)
            if t >= 1:
                e.wait_ge(p.sems["lncp_" + tag], t)
            e.activation(out=xsc[:, r, :], in_=X[:, t, :], func=AF.Square,
                         accum_out=st[:, 0:1]).then_inc(p.sems["sq_" + tag], 1)
        _, v_red = p.inc("red_" + tag)
        @p.on("vector")
        def _(e, t=t, v=wait, tag=tag):
            if t == 0 and v is not None:
                e.wait_ge(v[0], v[1])
            if t >= 2:
                e.wait_ge(p.sems["lncp_" + tag], t - 1)
            e.reduce_sum(st[:, 1:2], X[:, t, :], axis=AX.X).then_inc(p.sems["red_" + tag], 1)
        _, v_a1 = p.inc("a1_" + tag)
        @p.on("scalar")
        def _(e, v_red=v_red, tag=tag):
            e.wait_ge(p.sems["red_" + tag], v_red)
            e.activation(out=st[:, 2:3], in_=st[:, 1:2], func=AF.Copy,
                         scale=kc["neginvD"])
            e.activation(out=st[:, 4:5], in_=st[:, 2:3],
                         func=AF.Square).then_inc(p.sems["a1_" + tag], 1)
        _, v_d2 = p.inc("d2_" + tag)
        @p.on("vector")
        def _(e, v_sq=v_sq, v_a1=v_a1, tag=tag):
            e.wait_ge(p.sems["sq_" + tag], v_sq)
            e.wait_ge(p.sems["a1_" + tag], v_a1)
            e.scalar_tensor_tensor(st[:, 5:6], st[:, 0:1], 1.0 / D, st[:, 4:5],
                                   ALU.mult, ALU.subtract).then_inc(p.sems["d2_" + tag], 1)
        _, v_ln = p.inc("ln_" + tag)
        @p.on("scalar")
        def _(e, t=t, r=r, v_d2=v_d2, tag=tag):
            e.wait_ge(p.sems["d2_" + tag], v_d2)
            e.activation(out=st[:, 6:7], in_=st[:, 5:6], func=AF.Ln,
                         bias=kc["eps"])
            e.activation(out=st[:, 7:8], in_=st[:, 6:7], func=AF.Exp,
                         scale=kc["negh"]).then_inc(p.sems["ln_" + tag], 1)
        _, v_cp = p.inc("lncp_" + tag)
        @p.on("vector")
        def _(e, t=t, r=r, v_ln=v_ln, add=add_half_to_stream, tag=tag, km=keep_tokmaj):
            e.wait_ge(p.sems["ln_" + tag], v_ln)
            if t >= 4:
                e.wait_ge(p.sems["tr_" + tag], t - 3)
            ins = [e.tensor_scalar(xsc[:, r, :], X[:, t, :], st[:, 2:3], st[:, 7:8],
                                   ALU.add, ALU.mult),
                   e.tensor_copy(XLNROT[:, t % 4, :], xsc[:, r, :])]
            if km is not None:
                ins.append(e.tensor_copy(km[:, t, :], xsc[:, r, :]))
            if add:
                ins.append(e.scalar_tensor_tensor(X[:, t, :], xsc[:, r, :], 0.5,
                                                  X[:, t, :], ALU.mult, ALU.add))
            ins[-1].then_inc(p.sems["lncp_" + tag], 1)
        n += 1
        _, v_tr = p.inc("tr_" + tag)
        @p.on("tensor")
        def _(e, t=t, n=n, tag=tag):
            rr = t % 2
            if n == 1 and wait_consts is not None:
                e.wait_ge(wait_consts[0], wait_consts[1])
            e.wait_ge(p.sems["lncp_" + tag], t + 1)
            if n > 2:
                e.wait_ge(p.sems["xt_" + tag], n - 2)
            for c in range(DC):
                e.transpose(pst[:, rr, c * 128:(c + 1) * 128],
                            XLNROT[:, t % 4, c * 128:(c + 1) * 128],
                            ident[:, :])
            e.drain()
            e.nop().then_inc(p.sems["tr_" + tag], 1)
        _, v_ev = p.inc("xt_" + tag)
        @p.on("vector")
        def _(e, t=t, v_tr=v_tr, tag=tag):
            rr = t % 2
            e.wait_ge(p.sems["tr_" + tag], v_tr)
            e.tensor_copy(XT[:, :, t * 128:(t + 1) * 128],
                          pst[:, rr, :].rearrange("p (c f) -> p c f", c=DC))
            p.sig(e, "xt_" + tag)
    return "xt_" + tag, n


def emit_gate0(p, XT, GW, CW, iota8, kc, psg, G, gs, scale=0.5, dbg=None):
    for t in range(NT):
        _, v_g = p.inc("g_mm")
        @p.on("tensor")
        def _(e, t=t):
            r = t % 2
            e.wait_ge(p.sems["xt_l"], t + 1)
            if t >= 2:
                e.wait_ge(p.sems["g_ex"], t - 1)
            for c in range(DC):
                p.mm(e, psg[:, r, 0:8], XT[:, c, t * 128:(t + 1) * 128], GW[:, c, :],
                     16, start=(c == 0), stop=(c == DC - 1))
            e.drain()
            e.nop().then_inc(p.sems["g_mm"], 1)
        _, v_ge = p.inc("g_ev")
        @p.on("vector")
        def _(e, t=t, v_g=v_g):
            r = t % 2
            e.wait_ge(p.sems["g_mm"], v_g)
            if t >= 2:
                e.wait_ge(p.sems["g_cw"], t - 1)
            e.reduce_max(gs[:, 0:1], psg[:, r, 0:8], axis=AX.X, negate=True)
            p.sig(e, "g_ev")
        _, v_gx = p.inc("g_ex")
        @p.on("scalar")
        def _(e, t=t, v_ge=v_ge):
            e.wait_ge(p.sems["g_ev"], v_ge)
            if t >= 1:
                e.wait_ge(p.sems["g_p4"], t)
            e.activation(out=G[:, 1, :], in_=psg[:, t % 2, 0:8], func=AF.Exp,
                         bias=gs[:, 0:1], accum_out=gs[:, 1:2])
            p.sig(e, "g_ex")
        _, v_m1 = p.inc("g_m1")
        @p.on("vector")
        def _(e, v_gx=v_gx):
            e.wait_ge(p.sems["g_ex"], v_gx)
            e.reduce_max(gs[:, 2:3], G[:, 1, :], axis=AX.X)
            p.sig(e, "g_m1")
        _, v_p1 = p.inc("g_p1")
        @p.on("gpsimd")
        def _(e, v_m1=v_m1):
            EX = G[:, 1, :]
            e.wait_ge(p.sems["g_m1"], v_m1)
            e.tensor_scalar(G[:, 2, :], EX, gs[:, 2:3], None, ALU.is_equal)
            e.tensor_scalar(G[:, 3, :], G[:, 2, :], -9.0, None, ALU.mult)
            e.tensor_tensor(G[:, 3, :], G[:, 3, :], iota8[:, :], ALU.add)
            p.sig(e, "g_p1")
        _, v_i1 = p.inc("g_i1")
        @p.on("vector")
        def _(e, v_p1=v_p1):
            e.wait_ge(p.sems["g_p1"], v_p1)
            e.tensor_reduce(gs[:, 3:4], G[:, 3, :], axis=AX.X, op=ALU.min)
            p.sig(e, "g_i1")
        _, v_p2 = p.inc("g_p2")
        @p.on("gpsimd")
        def _(e, v_i1=v_i1):
            EX = G[:, 1, :]
            e.wait_ge(p.sems["g_i1"], v_i1)
            e.tensor_scalar(G[:, 4, :], iota8[:, :], gs[:, 3:4], None, ALU.subtract)
            e.tensor_scalar(G[:, 2, :], G[:, 4, :], 9.0, None, ALU.is_equal)
            e.tensor_scalar(G[:, 5, :], G[:, 2, :], -2.0, None, ALU.mult)
            e.tensor_tensor(G[:, 5, :], G[:, 5, :], EX, ALU.add)
            p.sig(e, "g_p2")
        _, v_m2 = p.inc("g_m2")
        @p.on("vector")
        def _(e, v_p2=v_p2):
            e.wait_ge(p.sems["g_p2"], v_p2)
            e.reduce_max(gs[:, 4:5], G[:, 5, :], axis=AX.X)
            p.sig(e, "g_m2")
        _, v_p3 = p.inc("g_p3")
        @p.on("gpsimd")
        def _(e, v_m2=v_m2):
            e.wait_ge(p.sems["g_m2"], v_m2)
            e.tensor_scalar(G[:, 6, :], G[:, 5, :], gs[:, 4:5], None, ALU.is_equal)
            e.tensor_scalar(G[:, 6, :], G[:, 6, :], -9.0, None, ALU.mult)
            e.tensor_tensor(G[:, 6, :], G[:, 6, :], iota8[:, :], ALU.add)
            p.sig(e, "g_p3")
        _, v_i2 = p.inc("g_i2")
        @p.on("vector")
        def _(e, v_p3=v_p3):
            e.wait_ge(p.sems["g_p3"], v_p3)
            e.tensor_reduce(gs[:, 5:6], G[:, 6, :], axis=AX.X, op=ALU.min)
            p.sig(e, "g_i2")
        _, v_p4 = p.inc("g_p4")
        @p.on("gpsimd")
        def _(e, v_i2=v_i2):
            EX = G[:, 1, :]
            e.wait_ge(p.sems["g_i2"], v_i2)
            e.tensor_scalar(G[:, 4, :], iota8[:, :], gs[:, 5:6], None, ALU.subtract)
            e.tensor_scalar(G[:, 3, :], G[:, 4, :], 9.0, None, ALU.is_equal)
            e.tensor_tensor(gs[:, 6:7], gs[:, 2:3], gs[:, 4:5], ALU.add)
            e.tensor_scalar(gs[:, 7:8], gs[:, 1:2], 1e-8, None, ALU.mult)
            e.tensor_tensor(gs[:, 6:7], gs[:, 6:7], gs[:, 7:8], ALU.add)
            e.tensor_tensor(G[:, 2, :], G[:, 2, :], G[:, 3, :], ALU.add)
            e.tensor_tensor(G[:, 2, :], G[:, 2, :], EX, ALU.mult)
            p.sig(e, "g_p4")
        _, v_r = p.inc("g_r")
        @p.on("vector")
        def _(e, v_p4=v_p4):
            e.wait_ge(p.sems["g_p4"], v_p4)
            e.reciprocal(gs[:, 7:8], gs[:, 6:7])
            p.sig(e, "g_r")
        _, v_cw = p.inc("g_cw")
        @p.on("gpsimd")
        def _(e, t=t, v_r=v_r, scale=scale):
            e.wait_ge(p.sems["g_r"], v_r)
            e.tensor_scalar(CW[:, t, :], G[:, 2, :], gs[:, 7:8], scale,
                            ALU.mult, ALU.mult)
            if dbg is not None:
                e.tensor_copy(dbg["EX"][:, t, :], G[:, 1, :])
                e.tensor_copy(dbg["GM"][:, t, :], G[:, 5, :])
                e.tensor_copy(dbg["M"][:, t, 0:1], gs[:, 2:3])
                e.tensor_copy(dbg["M"][:, t, 1:2], gs[:, 4:5])
                e.tensor_copy(dbg["M"][:, t, 2:3], gs[:, 3:4])
                e.tensor_copy(dbg["M"][:, t, 3:4], gs[:, 5:6])
                e.tensor_copy(dbg["M"][:, t, 4:5], gs[:, 6:7])
            p.sig(e, "g_cw")
    return "g_cw"


def build_ffn0(stages="full", nexp=8):
    nc = new_nc()
    x = nc.dram_tensor("x", [TSH, D], F32, kind="ExternalInput").ap()
    gw = nc.dram_tensor("gw", [D, 8], F32, kind="ExternalInput").ap()
    w1 = nc.dram_tensor("w1", [8, D, 2048], F32, kind="ExternalInput").ap()
    b1 = nc.dram_tensor("b1", [8, 2048], F32, kind="ExternalInput").ap()
    w2 = nc.dram_tensor("w2", [8, 2048, D], F32, kind="ExternalInput").ap()
    b2 = nc.dram_tensor("b2", [8, D], F32, kind="ExternalInput").ap()
    ident_in = nc.dram_tensor("ident", [128, 128], F32, kind="ExternalInput").ap()
    iota8_in = nc.dram_tensor("iota8", [128, 8], F32, kind="ExternalInput").ap()
    y = nc.dram_tensor("y", [TSH, D], F32, kind="ExternalOutput").ap()
    dbg_cw = nc.dram_tensor("dbg_cw", [TSH, 8], F32, kind="ExternalOutput").ap() if DEBUG_CW else None
    dbg_ex = nc.dram_tensor("dbg_ex", [TSH, 8], F32, kind="ExternalOutput").ap() if DEBUG_CW else None
    dbg_gm = nc.dram_tensor("dbg_gm", [TSH, 8], F32, kind="ExternalOutput").ap() if DEBUG_CW else None
    dbg_m = nc.dram_tensor("dbg_m", [TSH, 5], F32, kind="ExternalOutput").ap() if DEBUG_CW else None

    p = Prog(nc)
    X = p.sb("X", [128, NT, D], F32)
    XLNROT = p.sb("XLNROT", [128, 4, D], BF16)
    XT = p.sb("XT", [128, DC, TSH], BF16)
    ident = p.sb("ident_sb", [128, 128], BF16)
    iota8 = p.sb("iota8_sb", [128, 8], F32)
    GW = p.sb("GW", [128, DC, 8], BF16)
    CW = p.sb("CW", [128, NT, 8], F32)
    W1s = p.sb("W1s", [128, DC, 2048], BF16)
    B1row = p.sb("B1row", [1, 2048], BF16)
    W2s = p.sb("W2s", [128, 16, D], BF16)
    B2row = p.sb("B2row", [1, D], BF16)
    H = p.sb("H", [128, 16, 512], BF16)
    YS = p.sb("YS", [128, 2, 512], F32)
    sc = {
        "xln_f32": p.sb("xlnf", [128, 1, D], F32),
        "stats": p.sb("stats", [128, 16], F32),
    }
    G = p.sb("G", [128, 8, 8], F32)
    gs = p.sb("gs", [128, 8], F32)
    psg = p.ps("psg", [128, 2, 512], F32)
    psh = p.ps("psh", [128, 2, 512], F32)
    psy = p.ps("psy", [128, 2, 512], F32)

    kc = emit_consts(p)
    ones = kc["ones"]

    for s in ("s_x", "s_c", "s_y", "g_cw"):
        p.sem(s)
    @p.on("sync")
    def _(e):
        e.dma_start(out=X[:, :, :], in_=x.rearrange("(t p) d -> p t d", p=128)).then_inc(p.sems["s_x"], 16)
    p.cnt["s_x"] = 16
    @p.on("gpsimd")
    def _(e):
        e.dma_start(out=ident[:, :], in_=ident_in).then_inc(p.sems["s_c"], 16)
        e.dma_start(out=iota8[:, :], in_=iota8_in).then_inc(p.sems["s_c"], 16)
        e.dma_start(out=GW[:, :, :], in_=gw.rearrange("(c p) e -> p c e", p=128)).then_inc(p.sems["s_c"], 16)
    p.cnt["s_c"] = 48

    emit_ln_transpose(p, X, XT, XLNROT, ident, sc, kc, "l",
                      add_half_to_stream=True, wait=(p.sems["s_x"], 16),
                      wait_consts=(p.sems["s_c"], 48))

    dbg = {"EX": p.sb("dbgEX", [128, NT, 8], F32),
           "GM": p.sb("dbgGM", [128, NT, 8], F32),
           "M": p.sb("dbgM", [128, NT, 8], F32)} if DEBUG_CW else None
    if stages != "ln":
        emit_gate0(p, XT, GW, CW, iota8, kc, psg, G, gs, scale=0.5, dbg=dbg)

    for s in ("s_w", "h_mm", "h_ev", "y_mm", "y_sc", "y_ev"):
        p.sem(s)
    for ex in (range(nexp) if stages in ("full", "honly", "wload", "h_nob1") else []):
        @p.on("gpsimd")
        def _(e, ex=ex):
            if ex > 0:
                e.wait_ge(p.sems["y_mm"], ex * 32)
            e.dma_start(out=W1s[:, :, :], in_=w1[ex].rearrange("(c p) h -> p c h", p=128)).then_inc(p.sems["s_w"], 16)
            e.dma_start(out=B1row[0:1, :], in_=b1[ex][None, :]).then_inc(p.sems["s_w"], 16)
            e.dma_start(out=W2s[:, :, :], in_=w2[ex].rearrange("(c p) d -> p c d", p=128)).then_inc(p.sems["s_w"], 16)
            e.dma_start(out=B2row[0:1, :], in_=b2[ex][None, :]).then_inc(p.sems["s_w"], 16)
        p.cnt["s_w"] += 64
        for tb in (range(4) if stages != "wload" else []):
            for hc in range(16):
                n_h = ex * 64 + tb * 16 + hc
                _, v_h = p.inc("h_mm")
                @p.on("tensor")
                def _(e, ex=ex, tb=tb, hc=hc, n_h=n_h):
                    r = n_h % 2
                    if hc == 0 and tb == 0:
                        e.wait_ge(p.sems["s_w"], ex * 64 + 64)
                        e.wait_ge(p.sems["s_k"], 1)
                        if ex == 0:
                            e.wait_ge(p.sems["xt_l"], NT)
                    if n_h >= 2:
                        e.wait_ge(p.sems["h_ev"], n_h - 1)
                    nob1 = (stages == "h_nob1")
                    for c in range(DC):
                        p.mm(e, psh[:, r, :],
                             W1s[:, c, hc * 128:(hc + 1) * 128],
                             XT[:, c, tb * 512:(tb + 1) * 512],
                             1024, start=(c == 0), stop=(nob1 and c == DC - 1))
                    if not nob1:
                        p.mm(e, psh[:, r, :], B1row[0:1, hc * 128:(hc + 1) * 128],
                             ones[0:1, 0:512], 1024, start=False, stop=True)
                    e.drain()
                    e.nop().then_inc(p.sems["h_mm"], 1)
                _, v_he = p.inc("h_ev")
                @p.on("scalar")
                def _(e, ex=ex, tb=tb, hc=hc, n_h=n_h, v_h=v_h):
                    r = n_h % 2
                    e.wait_ge(p.sems["h_mm"], v_h)
                    if hc == 0 and stages not in ("honly", "h_nob1"):
                        e.wait_ge(p.sems["y_mm"], ex * 32 + tb * 8)
                    e.activation(out=H[:, hc, :], in_=psh[:, r, :], func=AF.Relu)
                    p.sig(e, "h_ev")
            for ts_ in (range(4) if stages != "honly" else []):
                tile = tb * 4 + ts_
                for dh in range(2):
                    n_y = ex * 32 + tb * 8 + ts_ * 2 + dh
                    _, v_y = p.inc("y_mm")
                    @p.on("tensor")
                    def _(e, ex=ex, tb=tb, ts_=ts_, dh=dh, n_y=n_y):
                        r = n_y % 2
                        e.wait_ge(p.sems["h_ev"], ex * 64 + tb * 16 + 16)
                        if n_y >= 2:
                            e.wait_ge(p.sems["y_sc"], n_y - 1)
                        for hc in range(16):
                            p.mm(e, psy[:, r, :],
                                 H[:, hc, ts_ * 128:(ts_ + 1) * 128],
                                 W2s[:, hc, dh * 512:(dh + 1) * 512],
                                 1024, start=(hc == 0), stop=False)
                        p.mm(e, psy[:, r, :], ones[0:1, 0:128],
                             B2row[0:1, dh * 512:(dh + 1) * 512],
                             1024, start=False, stop=True)
                        e.drain()
                        e.nop().then_inc(p.sems["y_mm"], 1)
                    _, v_ys = p.inc("y_sc")
                    @p.on("scalar")
                    def _(e, ex=ex, tile=tile, n_y=n_y, v_y=v_y):
                        r = n_y % 2
                        e.wait_ge(p.sems["y_mm"], v_y)
                        if n_y == 0:
                            e.wait_ge(p.sems["g_cw"], NT)
                        if n_y >= 2:
                            e.wait_ge(p.sems["y_ev"], n_y - 1)
                        e.activation(out=YS[:, r, :], in_=psy[:, r, :], func=AF.Copy,
                                     scale=CW[:, tile, ex:ex + 1])
                        p.sig(e, "y_sc")
                    _, v_ye = p.inc("y_ev")
                    @p.on("vector")
                    def _(e, tile=tile, dh=dh, n_y=n_y, v_ys=v_ys):
                        r = n_y % 2
                        e.wait_ge(p.sems["y_sc"], v_ys)
                        e.tensor_add(X[:, tile, dh * 512:(dh + 1) * 512],
                                     X[:, tile, dh * 512:(dh + 1) * 512],
                                     YS[:, r, :])
                        p.sig(e, "y_ev")

    _sy_target = [0]
    if DEBUG_CW:
        @p.on("sync")
        def _(e):
            e.wait_ge(p.sems["g_cw"], NT)
            e.dma_start(out=dbg_cw.rearrange("(t p) e -> p t e", p=128), in_=CW[:, :, :]).then_inc(p.sems["s_y"], 16)
            e.dma_start(out=dbg_ex.rearrange("(t p) e -> p t e", p=128), in_=dbg["EX"][:, :, :]).then_inc(p.sems["s_y"], 16)
            e.dma_start(out=dbg_gm.rearrange("(t p) e -> p t e", p=128), in_=dbg["GM"][:, :, :]).then_inc(p.sems["s_y"], 16)
            e.dma_start(out=dbg_m.rearrange("(t p) e -> p t e", p=128), in_=dbg["M"][:, :, 0:5]).then_inc(p.sems["s_y"], 16)
        p.cnt["s_y"] = 64
    @p.on("sync")
    def _(e):
        if stages == "full":
            e.wait_ge(p.sems["y_ev"], p.cnt["y_ev"])
        elif stages in ("honly", "h_nob1"):
            e.wait_ge(p.sems["h_ev"], p.cnt["h_ev"])
        elif stages == "wload":
            e.wait_ge(p.sems["s_w"], p.cnt["s_w"])
            e.wait_ge(p.sems["g_cw"], NT)
        elif stages == "lngate":
            e.wait_ge(p.sems["g_cw"], NT)
            e.wait_ge(p.sems["xt_l"], NT)
        else:
            e.wait_ge(p.sems["lncp_l"], NT)
            e.wait_ge(p.sems["xt_l"], NT)
        e.dma_start(out=y.rearrange("(t p) d -> p t d", p=128), in_=X[:, :, :]).then_inc(p.sems["s_y"], 16)
        e.wait_ge(p.sems["s_y"], _sy_target[0])
    p.cnt["s_y"] += 16
    _sy_target[0] = p.cnt["s_y"]

    p.finish()
    return nc


def _ident_np():
    return np.eye(128, dtype=np.float32)


def _iota8_np():
    return np.tile(np.arange(8, dtype=np.float32)[None, :], (128, 1))


_NC_CACHE = {}


def _get_nc(name, builder):
    if name not in _NC_CACHE:
        _NC_CACHE[name] = builder()
    return _NC_CACHE[name]


def run_ffn0(x_full, gw, w1, b1, w2, b2):
    nc = _get_nc("ffn0", build_ffn0)
    ident = _ident_np()
    iota8 = _iota8_np()
    in_maps = []
    for c in range(NCORE):
        in_maps.append({
            "x": np.ascontiguousarray(x_full[c * TSH:(c + 1) * TSH]),
            "gw": gw, "w1": w1, "b1": b1, "w2": w2, "b2": b2,
            "ident": ident, "iota8": iota8,
        })
    res = run_bass_kernel_spmd(nc, in_maps, core_ids=list(range(NCORE)))
    return np.concatenate([res.results[c]["y"] for c in range(NCORE)], axis=0)


def build_qkv0():
    nc = new_nc()
    x = nc.dram_tensor("x", [TSH, D], F32, kind="ExternalInput").ap()
    wqkv = nc.dram_tensor("wqkv", [D, 3 * D], F32, kind="ExternalInput").ap()
    ident_in = nc.dram_tensor("ident", [128, 128], F32, kind="ExternalInput").ap()
    qf = nc.dram_tensor("qf", [D, TSH], BF16, kind="ExternalOutput").ap()
    kf = nc.dram_tensor("kf", [D, TSH], BF16, kind="ExternalOutput").ap()
    vt = nc.dram_tensor("vt", [TSH, D], BF16, kind="ExternalOutput").ap()

    p = Prog(nc)
    X = p.sb("X", [128, NT, D], F32)
    XLNROT = p.sb("XLNROT", [128, 4, D], BF16)
    XT = p.sb("XT", [128, DC, TSH], BF16)
    ident = p.sb("ident_sb", [128, 128], BF16)
    WQ = p.sb("WQ", [128, DC, D], BF16)
    WK = p.sb("WK", [128, DC, D], BF16)
    WV = p.sb("WV", [128, DC, D], BF16)
    STG = p.sb("STG", [128, 4, 512], BF16)
    sc = {"xln_f32": p.sb("xlnf", [128, 1, D], F32),
          "stats": p.sb("stats", [128, 16], F32)}
    psq = p.ps("psq", [128, 2, 512], F32)
    kc = emit_consts(p)

    for s in ("s_x", "s_c", "s_w", "s_st", "q_mm", "q_ev"):
        p.sem(s)
    @p.on("sync")
    def _(e):
        e.dma_start(out=X[:, :, :], in_=x.rearrange("(t p) d -> p t d", p=128)).then_inc(p.sems["s_x"], 16)
    p.cnt["s_x"] = 16
    @p.on("gpsimd")
    def _(e):
        e.dma_start(out=ident[:, :], in_=ident_in).then_inc(p.sems["s_c"], 16)
        e.dma_start(out=WQ[:, :, :], in_=wqkv[:, 0:D].rearrange("(c p) n -> p c n", p=128)).then_inc(p.sems["s_w"], 16)
        e.dma_start(out=WK[:, :, :], in_=wqkv[:, D:2 * D].rearrange("(c p) n -> p c n", p=128)).then_inc(p.sems["s_w"], 16)
        e.dma_start(out=WV[:, :, :], in_=wqkv[:, 2 * D:3 * D].rearrange("(c p) n -> p c n", p=128)).then_inc(p.sems["s_w"], 16)
    p.cnt["s_c"] = 16
    p.cnt["s_w"] = 48

    emit_ln_transpose(p, X, XT, XLNROT, ident, sc, kc, "l",
                      add_half_to_stream=False, wait=(p.sems["s_x"], 16),
                      wait_consts=(p.sems["s_c"], 16))

    qf3 = qf.rearrange("(n p) t -> p n t", p=128)
    kf3 = kf.rearrange("(n p) t -> p n t", p=128)
    vt3 = vt.rearrange("(t p) d -> p t d", p=128)
    n_mm = 0
    for which, W, dst in (("q", WQ, qf3), ("k", WK, kf3)):
        for nch in range(DC):
            for ts_ in range(4):
                n_mm += 1
                _, v_q = p.inc("q_mm")
                @p.on("tensor")
                def _(e, W=W, nch=nch, ts_=ts_, n_mm=n_mm):
                    r = n_mm % 2
                    if n_mm == 1:
                        e.wait_ge(p.sems["s_w"], 48)
                        e.wait_ge(p.sems["xt_l"], NT)
                    if n_mm > 2:
                        e.wait_ge(p.sems["q_ev"], n_mm - 2)
                    for c in range(DC):
                        e.matmul(psq[:, r, :], W[:, c, nch * 128:(nch + 1) * 128],
                                 XT[:, c, ts_ * 512:(ts_ + 1) * 512],
                                 start=(c == 0), stop=(c == DC - 1))
                    e.drain()
                    e.nop().then_inc(p.sems["q_mm"], 1)
                _, v_qe = p.inc("q_ev")
                @p.on("vector")
                def _(e, nch=nch, ts_=ts_, n_mm=n_mm, v_q=v_q):
                    r = n_mm % 2
                    sr = n_mm % 4
                    e.wait_ge(p.sems["q_mm"], v_q)
                    if n_mm > 4:
                        e.wait_ge(p.sems["s_st"], (n_mm - 4) * 16)
                    e.tensor_copy(STG[:, sr, :], psq[:, r, :])
                    p.sig(e, "q_ev")
                _, v_st = p.inc("s_st", 16)
                @p.on("sync")
                def _(e, dst=dst, nch=nch, ts_=ts_, n_mm=n_mm, v_qe=v_qe):
                    sr = n_mm % 4
                    e.wait_ge(p.sems["q_ev"], v_qe)
                    e.dma_start(out=dst[:, nch, ts_ * 512:(ts_ + 1) * 512],
                                in_=STG[:, sr, :]).then_inc(p.sems["s_st"], 16)
    for t_ in range(NT):
        for dh in range(2):
            n_mm += 1
            _, v_q = p.inc("q_mm")
            @p.on("tensor")
            def _(e, t_=t_, dh=dh, n_mm=n_mm):
                r = n_mm % 2
                if n_mm > 2:
                    e.wait_ge(p.sems["q_ev"], n_mm - 2)
                for c in range(DC):
                    e.matmul(psq[:, r, :], XT[:, c, t_ * 128:(t_ + 1) * 128],
                             WV[:, c, dh * 512:(dh + 1) * 512],
                             start=(c == 0), stop=(c == DC - 1))
                e.drain()
                e.nop().then_inc(p.sems["q_mm"], 1)
            _, v_qe = p.inc("q_ev")
            @p.on("vector")
            def _(e, n_mm=n_mm, v_q=v_q):
                r = n_mm % 2
                sr = n_mm % 4
                e.wait_ge(p.sems["q_mm"], v_q)
                if n_mm > 4:
                    e.wait_ge(p.sems["s_st"], (n_mm - 4) * 16)
                e.tensor_copy(STG[:, sr, :], psq[:, r, :])
                p.sig(e, "q_ev")
            _, v_st = p.inc("s_st", 16)
            @p.on("sync")
            def _(e, t_=t_, dh=dh, n_mm=n_mm, v_qe=v_qe):
                sr = n_mm % 4
                e.wait_ge(p.sems["q_ev"], v_qe)
                e.dma_start(out=vt3[:, t_, dh * 512:(dh + 1) * 512],
                            in_=STG[:, sr, :]).then_inc(p.sems["s_st"], 16)
    @p.on("sync")
    def _(e):
        e.wait_ge(p.sems["s_st"], p.cnt["s_st"])
    p.finish()
    return nc


ATT_NH = 4
ATT_KV = T_FULL


def build_attn():
    NP = ATT_NH // 2
    KB5 = ATT_KV // 512
    KB1 = ATT_KV // 128
    nc = new_nc()
    qf = nc.dram_tensor("qf", [ATT_NH * 64, TSH], BF16, kind="ExternalInput").ap()
    kfull = nc.dram_tensor("kfull", [ATT_NH * 64, ATT_KV], BF16, kind="ExternalInput").ap()
    vfull = nc.dram_tensor("vfull", [ATT_KV, ATT_NH * 64], BF16, kind="ExternalInput").ap()
    qpos_in = nc.dram_tensor("qpos", [128, 16], F32, kind="ExternalInput").ap()
    iot_in = nc.dram_tensor("iot512", [128, 512], F32, kind="ExternalInput").ap()
    ident_in = nc.dram_tensor("ident", [128, 128], F32, kind="ExternalInput").ap()
    of = nc.dram_tensor("of", [ATT_NH * 64, TSH], BF16, kind="ExternalOutput").ap()

    p = Prog(nc)
    QF = p.sb("QF", [128, NP, TSH], BF16)
    KP = p.sb("KP", [128, ATT_KV], BF16)
    VP = p.sb("VP", [128, KB1, 130], BF16)
    SA = p.sb("SAs", [128, ATT_KV], F32)
    SB = p.sb("SBs", [128, ATT_KV], F32)
    P_ = p.sb("Ps", [128, ATT_KV], BF16)
    PTs = p.sb("PTs", [128, 2, 512], BF16)
    OFs = p.sb("OFs", [128, NP, TSH], BF16)
    IOT = p.sb("IOT", [128, 512], F32)
    QPOS = p.sb("QPOS", [128, 16], F32)
    IDB = p.sb("IDB", [128, 128], BF16)
    MS = p.sb("MS", [128, 2, 512], F32)
    BM = p.sb("BM", [128, 16], F32)
    SM = p.sb("SM", [128, 16], F32)
    RR = p.sb("RR", [1, 512], F32)
    RB = p.sb("RB", [64, 512], F32)
    ONE32 = p.sb("ONE32", [1, 64], F32)
    kc = emit_consts(p)
    ones = kc["ones"]
    psS = p.ps("psS", [128, 4, 512], F32)
    psPT = p.ps("psPT", [128, 2, 1024], BF16)
    psO = p.ps("psO", [128, 2, 512], F32)

    for s_ in ("s_q", "s_kv", "s_mm", "s_sev", "mk_", "bm_", "bmb_", "ngm_",
               "p_ex", "pt_mm", "pt_ev", "o_mm", "o_rr", "o_bc", "o_bce",
               "o_ev", "s_o", "s_fin"):
        p.sem(s_)
    @p.on("sync")
    def _(e):
        e.dma_start(out=QF[:, :, :], in_=qf.rearrange("(n p) t -> p n t", p=128)).then_inc(p.sems["s_q"], 16)
        e.dma_start(out=IOT[:, :], in_=iot_in).then_inc(p.sems["s_q"], 16)
        e.dma_start(out=QPOS[:, :], in_=qpos_in).then_inc(p.sems["s_q"], 16)
    p.cnt["s_q"] = 48
    @p.on("gpsimd")
    def _(e):
        e.dma_start(out=IDB[:, :], in_=ident_in).then_inc(p.sems["s_kv"], 16)
    p.cnt["s_kv"] = 16
    @p.on("vector")
    def _(e):
        e.memset(VP[:, :, 64:65], 1.0)
        e.memset(VP[:, :, 129:130], 1.0)
        e.memset(ONE32[:, :], 1.0)
        p.sig(e, "s_o")

    kf3 = kfull.rearrange("(n p) t -> p n t", p=128)
    vt3 = vfull.rearrange("(kb p) f -> p kb f", p=128)

    C = {"smm": 0, "sev": 0, "mk": 0, "bm": 0, "ngm": 0, "pex": 0,
         "ptmm": 0, "ptev": 0, "omm": 0, "orr": 0, "obc": 0, "obce": 0, "oev": 0}
    for cpair in range(NP):
        oev_before_pair = C["oev"]
        _, v_kv = p.inc("s_kv", 48)
        @p.on("gpsimd")
        def _(e, cpair=cpair, oev_b=oev_before_pair):
            if cpair > 0:
                e.wait_ge(p.sems["o_ev"], oev_b)
            e.dma_start(out=KP[:, :], in_=kf3[:, cpair, :]).then_inc(p.sems["s_kv"], 16)
            e.dma_start(out=VP[:, :, 0:64],
                        in_=vt3[:, :, cpair * 128:cpair * 128 + 64]).then_inc(p.sems["s_kv"], 16)
            e.dma_start(out=VP[:, :, 65:129],
                        in_=vt3[:, :, cpair * 128 + 64:cpair * 128 + 128]).then_inc(p.sems["s_kv"], 16)
        for qt in range(NT):
            for kb in range(KB5):
                C["mk"] += 1
                _, v_mk = p.inc("mk_")
                @p.on("gpsimd")
                def _(e, qt=qt, kb=kb, n=C["mk"]):
                    if n == 1:
                        e.wait_ge(p.sems["s_q"], 48)
                    if n > 2:
                        e.wait_ge(p.sems["s_sev"], n - 2)
                    e.tensor_scalar(SM[:, 15:16], QPOS[:, qt:qt + 1], float(kb * 512),
                                    None, ALU.subtract)
                    e.tensor_scalar(MS[:, n % 2, :], IOT[:, :], SM[:, 15:16],
                                    None, ALU.is_gt)
                    p.sig(e, "mk_")
                C["smm"] += 1
                _, v_sm = p.inc("s_mm")
                @p.on("tensor")
                def _(e, cpair=cpair, qt=qt, kb=kb, n=C["smm"], kvt=p.cnt["s_kv"]):
                    rA = kb % 2
                    rB = 2 + kb % 2
                    if qt == 0 and kb == 0:
                        e.wait_ge(p.sems["s_kv"], kvt)
                        if cpair == 0:
                            e.wait_ge(p.sems["s_q"], 48)
                    if n > 2:
                        e.wait_ge(p.sems["s_sev"], n - 2)
                    e.matmul(psS[:, rA, :], QF[0:64, cpair, qt * 128:(qt + 1) * 128],
                             KP[0:64, kb * 512:(kb + 1) * 512], start=True, stop=True)
                    e.matmul(psS[:, rB, :], QF[64:128, cpair, qt * 128:(qt + 1) * 128],
                             KP[64:128, kb * 512:(kb + 1) * 512], start=True, stop=True,
                             tile_position=(64, 0))
                    e.drain()
                    e.nop().then_inc(p.sems["s_mm"], 1)
                C["sev"] += 1
                _, v_se = p.inc("s_sev")
                @p.on("vector")
                def _(e, qt=qt, kb=kb, v_sm=v_sm, v_mk=v_mk):
                    rA = kb % 2
                    rB = 2 + kb % 2
                    mr = v_mk % 2
                    e.wait_ge(p.sems["s_mm"], v_sm)
                    e.wait_ge(p.sems["mk_"], v_mk)
                    e.scalar_tensor_tensor(SA[:, kb * 512:(kb + 1) * 512], MS[:, mr, :],
                                           -1e30, psS[:, rA, :], ALU.mult, ALU.add)
                    e.scalar_tensor_tensor(SB[:, kb * 512:(kb + 1) * 512], MS[:, mr, :],
                                           -1e30, psS[:, rB, :], ALU.mult, ALU.add)
                    p.sig(e, "s_sev")
            for hh, SS in ((0, SA), (1, SB)):
                C["bm"] += 1
                _, v_bm = p.inc("bm_")
                @p.on("vector")
                def _(e, SS=SS, sev_t=C["sev"]):
                    e.wait_ge(p.sems["s_sev"], sev_t)
                    e.drain()
                    for kb in range(KB5):
                        e.reduce_max(BM[:, kb:kb + 1], SS[:, kb * 512:(kb + 1) * 512],
                                     axis=AX.X)
                    p.sig(e, "bm_")
                _, v_bb = p.inc("bmb_")
                @p.on("scalar")
                def _(e, v_bm=v_bm):
                    e.wait_ge(p.sems["bm_"], v_bm)
                    e.nop().then_inc(p.sems["bmb_"], 1)
                C["ngm"] += 1
                _, v_ng = p.inc("ngm_")
                @p.on("vector")
                def _(e, v_bb=v_bb):
                    e.wait_ge(p.sems["bmb_"], v_bb)
                    e.tensor_max(SM[:, 0:8], BM[:, 0:8], BM[:, 8:16])
                    e.tensor_max(SM[:, 0:4], SM[:, 0:4], SM[:, 4:8])
                    e.tensor_max(SM[:, 0:2], SM[:, 0:2], SM[:, 2:4])
                    e.tensor_max(SM[:, 0:1], SM[:, 0:1], SM[:, 1:2])
                    e.tensor_scalar(SM[:, 8:9], SM[:, 0:1], -0.125, None, ALU.mult)
                    p.sig(e, "ngm_")
                C["pex"] += 1
                _, v_px = p.inc("p_ex")
                @p.on("scalar")
                def _(e, SS=SS, v_ng=v_ng, ptev_b=C["ptev"]):
                    e.wait_ge(p.sems["ngm_"], v_ng)
                    if ptev_b > 0:
                        e.wait_ge(p.sems["pt_ev"], ptev_b)
                    e.activation(out=P_[:, :], in_=SS[:, :], func=AF.Exp,
                                 bias=SM[:, 8:9], scale=kc["eighth"])
                    p.sig(e, "p_ex")
                for g4 in range(KB1 // 4):
                    C["ptmm"] += 1
                    _, v_pt = p.inc("pt_mm")
                    @p.on("tensor")
                    def _(e, g4=g4, n=C["ptmm"], v_px=v_px):
                        rr_ = n % 2
                        e.wait_ge(p.sems["p_ex"], v_px)
                        if n > 2:
                            e.wait_ge(p.sems["pt_ev"], n - 2)
                        for j in range(4):
                            kb1 = g4 * 4 + j
                            e.transpose(psPT[:, rr_, j * 128:(j + 1) * 128],
                                        P_[:, kb1 * 128:(kb1 + 1) * 128], IDB[:, :])
                        e.drain()
                        e.nop().then_inc(p.sems["pt_mm"], 1)
                    C["ptev"] += 1
                    _, v_pte = p.inc("pt_ev")
                    @p.on("vector")
                    def _(e, n=C["ptev"], v_pt=v_pt, omm_b=C["omm"]):
                        rr_ = n % 2
                        e.wait_ge(p.sems["pt_mm"], v_pt)
                        if n > 2:
                            e.wait_ge(p.sems["o_mm"], omm_b - 1)
                        e.tensor_copy(PTs[:, rr_, :], psPT[:, rr_, 0:512])
                        p.sig(e, "pt_ev")
                    C["omm"] += 1
                    _, v_om = p.inc("o_mm")
                    @p.on("tensor")
                    def _(e, hh=hh, g4=g4, nptev=C["ptev"], v_pte=v_pte,
                          oev_b=C["oev"]):
                        e.wait_ge(p.sems["pt_ev"], v_pte)
                        if g4 == 0:
                            if oev_b > 0:
                                e.wait_ge(p.sems["o_ev"], oev_b)
                            e.wait_ge(p.sems["s_o"], 1)
                        for j in range(4):
                            kb1 = g4 * 4 + j
                            e.matmul(psO[0:65, 0, 0:128],
                                     VP[:, kb1, hh * 65:hh * 65 + 65],
                                     PTs[:, nptev % 2, j * 128:(j + 1) * 128],
                                     start=(kb1 == 0), stop=(kb1 == KB1 - 1))
                        e.drain()
                        e.nop().then_inc(p.sems["o_mm"], 1)
                C["orr"] += 1
                _, v_or = p.inc("o_rr")
                @p.on("vector")
                def _(e, omm_t=C["omm"]):
                    e.wait_ge(p.sems["o_mm"], omm_t)
                    e.reciprocal(RR[0:1, 0:128], psO[64:65, 0, 0:128])
                    p.sig(e, "o_rr")
                C["obc"] += 1
                _, v_ob = p.inc("o_bc")
                @p.on("tensor")
                def _(e, v_or=v_or):
                    e.wait_ge(p.sems["o_rr"], v_or)
                    e.wait_ge(p.sems["s_o"], 1)
                    e.matmul(psO[0:64, 1, 0:128], ONE32[0:1, 0:64], RR[0:1, 0:128],
                             start=True, stop=True)
                    e.drain()
                    e.nop().then_inc(p.sems["o_bc"], 1)
                C["obce"] += 1
                _, v_obe = p.inc("o_bce")
                @p.on("vector")
                def _(e, v_ob=v_ob):
                    e.wait_ge(p.sems["o_bc"], v_ob)
                    e.tensor_copy(RB[0:64, 0:128], psO[0:64, 1, 0:128])
                    p.sig(e, "o_bce")
                C["oev"] += 1
                _, v_oe = p.inc("o_ev")
                @p.on("vector")
                def _(e, cpair=cpair, hh=hh, qt=qt, v_obe=v_obe):
                    e.wait_ge(p.sems["o_bce"], v_obe)
                    e.tensor_tensor(OFs[hh * 64:(hh + 1) * 64, cpair, qt * 128:(qt + 1) * 128],
                                    psO[0:64, 0, 0:128], RB[0:64, 0:128], ALU.mult)
                    p.sig(e, "o_ev")
    @p.on("sync")
    def _(e):
        e.wait_ge(p.sems["o_ev"], C["oev"])
        e.dma_start(out=of.rearrange("(n p) t -> p n t", p=128),
                    in_=OFs[:, :, :]).then_inc(p.sems["s_fin"], 16)
        e.wait_ge(p.sems["s_fin"], 16)
    p.cnt["s_fin"] = 16
    p.finish()
    return nc


def build_outproj():
    nc = new_nc()
    x = nc.dram_tensor("x", [TSH, D], F32, kind="ExternalInput").ap()
    ofin = nc.dram_tensor("ofin", [D, TSH], BF16, kind="ExternalInput").ap()
    wout = nc.dram_tensor("wout", [D, D], F32, kind="ExternalInput").ap()
    bout = nc.dram_tensor("bout", [D], F32, kind="ExternalInput").ap()
    y = nc.dram_tensor("y", [TSH, D], F32, kind="ExternalOutput").ap()

    p = Prog(nc)
    X = p.sb("X", [128, NT, D], F32)
    OF = p.sb("OF", [128, DC, TSH], BF16)
    WO = p.sb("WO", [128, DC, D], BF16)
    BO = p.sb("BO", [1, D], BF16)
    YS = p.sb("YS", [128, 2, 512], F32)
    kc = emit_consts(p)
    ones = kc["ones"]
    psy = p.ps("psy", [128, 2, 512], F32)

    for s_ in ("s_x", "s_w", "y_mm", "y_sc", "y_ev", "s_y"):
        p.sem(s_)
    @p.on("sync")
    def _(e):
        e.dma_start(out=X[:, :, :], in_=x.rearrange("(t p) d -> p t d", p=128)).then_inc(p.sems["s_x"], 16)
        e.dma_start(out=OF[:, :, :], in_=ofin.rearrange("(c p) t -> p c t", p=128)).then_inc(p.sems["s_x"], 16)
    p.cnt["s_x"] = 32
    @p.on("gpsimd")
    def _(e):
        e.dma_start(out=WO[:, :, :], in_=wout.rearrange("(c p) n -> p c n", p=128)).then_inc(p.sems["s_w"], 16)
        e.dma_start(out=BO[0:1, :], in_=bout[None, :]).then_inc(p.sems["s_w"], 16)
    p.cnt["s_w"] = 32

    n_y = 0
    for t_ in range(NT):
        for dh in range(2):
            n_y += 1
            _, v_y = p.inc("y_mm")
            @p.on("tensor")
            def _(e, t_=t_, dh=dh, n=n_y):
                r = n % 2
                if n == 1:
                    e.wait_ge(p.sems["s_x"], 32)
                    e.wait_ge(p.sems["s_w"], 32)
                    e.wait_ge(p.sems["s_k"], 1)
                if n > 2:
                    e.wait_ge(p.sems["y_sc"], n - 2)
                for c in range(DC):
                    e.matmul(psy[:, r, :], OF[:, c, t_ * 128:(t_ + 1) * 128],
                             WO[:, c, dh * 512:(dh + 1) * 512],
                             start=(c == 0), stop=False)
                e.matmul(psy[:, r, :], ones[0:1, 0:128],
                         BO[0:1, dh * 512:(dh + 1) * 512], start=False, stop=True)
                e.drain()
                e.nop().then_inc(p.sems["y_mm"], 1)
            _, v_ys = p.inc("y_sc")
            @p.on("scalar")
            def _(e, n=n_y, v_y=v_y):
                r = n % 2
                e.wait_ge(p.sems["y_mm"], v_y)
                if n > 2:
                    e.wait_ge(p.sems["y_ev"], n - 2)
                e.activation(out=YS[:, r, :], in_=psy[:, r, :], func=AF.Copy)
                p.sig(e, "y_sc")
            _, v_ye = p.inc("y_ev")
            @p.on("vector")
            def _(e, t_=t_, dh=dh, n=n_y, v_ys=v_ys):
                r = n % 2
                e.wait_ge(p.sems["y_sc"], v_ys)
                e.tensor_add(X[:, t_, dh * 512:(dh + 1) * 512],
                             X[:, t_, dh * 512:(dh + 1) * 512], YS[:, r, :])
                p.sig(e, "y_ev")
    @p.on("sync")
    def _(e):
        e.wait_ge(p.sems["y_ev"], p.cnt["y_ev"])
        e.dma_start(out=y.rearrange("(t p) d -> p t d", p=128), in_=X[:, :, :]).then_inc(p.sems["s_y"], 16)
        e.wait_ge(p.sems["s_y"], 16)
    p.cnt["s_y"] = 16
    p.finish()
    return nc


def build_gate1():
    nc = new_nc()
    x = nc.dram_tensor("x", [TSH, D], F32, kind="ExternalInput").ap()
    wr = nc.dram_tensor("wr", [D, 8], F32, kind="ExternalInput").ap()
    ident_in = nc.dram_tensor("ident", [128, 128], F32, kind="ExternalInput").ap()
    iota8_in = nc.dram_tensor("iota8", [128, 8], F32, kind="ExternalInput").ap()
    oh = nc.dram_tensor("oh", [TSH, 8], F32, kind="ExternalOutput").ap()
    pp = nc.dram_tensor("pp", [128, NT], F32, kind="ExternalOutput").ap()

    p = Prog(nc)
    X = p.sb("X", [128, NT, D], F32)
    XLNROT = p.sb("XLNROT", [128, 4, D], BF16)
    XT = p.sb("XT", [128, DC, TSH], BF16)
    ident = p.sb("ident_sb", [128, 128], BF16)
    iota8 = p.sb("iota8_sb", [128, 8], F32)
    GW = p.sb("GW", [128, DC, 8], BF16)
    OH = p.sb("OH", [128, NT, 8], F32)
    PP = p.sb("PP", [128, NT], F32)
    sc = {"xln_f32": p.sb("xlnf", [128, 1, D], F32),
          "stats": p.sb("stats", [128, 16], F32)}
    G = p.sb("G", [128, 8, 8], F32)
    gs = p.sb("gs", [128, 8], F32)
    psg = p.ps("psg", [128, 2, 512], F32)
    kc = emit_consts(p)

    for s_ in ("s_x", "s_c", "g_mm", "g_ev", "g_ex", "g_m1", "g_p1", "g_i1",
               "g_p2", "g_r", "g_pp", "s_y"):
        p.sem(s_)
    @p.on("sync")
    def _(e):
        e.dma_start(out=X[:, :, :], in_=x.rearrange("(t p) d -> p t d", p=128)).then_inc(p.sems["s_x"], 16)
    p.cnt["s_x"] = 16
    @p.on("gpsimd")
    def _(e):
        e.dma_start(out=ident[:, :], in_=ident_in).then_inc(p.sems["s_c"], 16)
        e.dma_start(out=iota8[:, :], in_=iota8_in).then_inc(p.sems["s_c"], 16)
        e.dma_start(out=GW[:, :, :], in_=wr.rearrange("(c p) e -> p c e", p=128)).then_inc(p.sems["s_c"], 16)
    p.cnt["s_c"] = 48

    emit_ln_transpose(p, X, XT, XLNROT, ident, sc, kc, "l",
                      add_half_to_stream=False, wait=(p.sems["s_x"], 16),
                      wait_consts=(p.sems["s_c"], 48))

    for t in range(NT):
        _, v_g = p.inc("g_mm")
        @p.on("tensor")
        def _(e, t=t):
            r = t % 2
            e.wait_ge(p.sems["xt_l"], t + 1)
            if t >= 2:
                e.wait_ge(p.sems["g_ex"], t - 1)
            for c in range(DC):
                e.matmul(psg[:, r, 0:8], XT[:, c, t * 128:(t + 1) * 128], GW[:, c, :],
                         start=(c == 0), stop=(c == DC - 1))
            e.drain()
            e.nop().then_inc(p.sems["g_mm"], 1)
        _, v_ge = p.inc("g_ev")
        @p.on("vector")
        def _(e, t=t, v_g=v_g):
            r = t % 2
            e.wait_ge(p.sems["g_mm"], v_g)
            if t >= 2:
                e.wait_ge(p.sems["g_r"], t - 1)
            e.reduce_max(gs[:, 0:1], psg[:, r, 0:8], axis=AX.X, negate=True)
            p.sig(e, "g_ev")
        _, v_gx = p.inc("g_ex")
        @p.on("scalar")
        def _(e, t=t, v_ge=v_ge):
            e.wait_ge(p.sems["g_ev"], v_ge)
            if t >= 1:
                e.wait_ge(p.sems["g_pp"], t)
            e.activation(out=G[:, 1, :], in_=psg[:, t % 2, 0:8], func=AF.Exp,
                         bias=gs[:, 0:1], accum_out=gs[:, 1:2])
            p.sig(e, "g_ex")
        _, v_m1 = p.inc("g_m1")
        @p.on("vector")
        def _(e, v_gx=v_gx):
            e.wait_ge(p.sems["g_ex"], v_gx)
            e.reduce_max(gs[:, 2:3], G[:, 1, :], axis=AX.X)
            p.sig(e, "g_m1")
        _, v_p1 = p.inc("g_p1")
        @p.on("gpsimd")
        def _(e, v_m1=v_m1):
            e.wait_ge(p.sems["g_m1"], v_m1)
            e.tensor_scalar(G[:, 2, :], G[:, 1, :], gs[:, 2:3], None, ALU.is_equal)
            e.tensor_scalar(G[:, 3, :], G[:, 2, :], -9.0, None, ALU.mult)
            e.tensor_tensor(G[:, 3, :], G[:, 3, :], iota8[:, :], ALU.add)
            p.sig(e, "g_p1")
        _, v_i1 = p.inc("g_i1")
        @p.on("vector")
        def _(e, v_p1=v_p1):
            e.wait_ge(p.sems["g_p1"], v_p1)
            e.tensor_reduce(gs[:, 3:4], G[:, 3, :], axis=AX.X, op=ALU.min)
            p.sig(e, "g_i1")
        _, v_p2 = p.inc("g_p2")
        @p.on("gpsimd")
        def _(e, t=t, v_i1=v_i1):
            e.wait_ge(p.sems["g_i1"], v_i1)
            e.tensor_scalar(G[:, 4, :], iota8[:, :], gs[:, 3:4], None, ALU.subtract)
            e.tensor_scalar(OH[:, t, :], G[:, 4, :], 9.0, None, ALU.is_equal)
            p.sig(e, "g_p2")
        _, v_r = p.inc("g_r")
        @p.on("vector")
        def _(e, t=t, v_p2=v_p2):
            e.wait_ge(p.sems["g_p2"], v_p2)
            e.reciprocal(gs[:, 4:5], gs[:, 1:2])
            p.sig(e, "g_r")
        _, v_pp = p.inc("g_pp")
        @p.on("gpsimd")
        def _(e, t=t, v_r=v_r):
            e.wait_ge(p.sems["g_r"], v_r)
            e.tensor_scalar(PP[:, t:t + 1], gs[:, 2:3], gs[:, 4:5], None, ALU.mult)
            p.sig(e, "g_pp")
    @p.on("sync")
    def _(e):
        e.wait_ge(p.sems["g_pp"], p.cnt["g_pp"])
        e.wait_ge(p.sems["g_p2"], p.cnt["g_p2"])
        e.dma_start(out=oh.rearrange("(t p) e -> p t e", p=128), in_=OH[:, :, :]).then_inc(p.sems["s_y"], 16)
        e.dma_start(out=pp, in_=PP[:, :]).then_inc(p.sems["s_y"], 16)
        e.wait_ge(p.sems["s_y"], 32)
    p.cnt["s_y"] = 32
    p.finish()
    return nc


def build_ffn1(skip_rank=False, rank_stage=4):
    F1 = 4096
    FC = F1 // 128
    NQ = 4
    CAP = 2048.0
    nc = new_nc()
    x = nc.dram_tensor("x", [TSH, D], F32, kind="ExternalInput").ap()
    w1 = nc.dram_tensor("w1", [8, D, F1], F32, kind="ExternalInput").ap()
    w2 = nc.dram_tensor("w2", [8, F1, D], F32, kind="ExternalInput").ap()
    ohf = nc.dram_tensor("ohf", [16384, 8], F32, kind="ExternalInput").ap()
    ohown = nc.dram_tensor("ohown", [TSH, 8], F32, kind="ExternalInput").ap()
    ppin = nc.dram_tensor("ppin", [128, NT], F32, kind="ExternalInput").ap()
    pmask = nc.dram_tensor("pmask", [16384, 1], F32, kind="ExternalInput").ap()
    ident_in = nc.dram_tensor("ident", [128, 128], F32, kind="ExternalInput").ap()
    lt_in = nc.dram_tensor("lt", [128, 128], F32, kind="ExternalInput").ap()
    y = nc.dram_tensor("y", [TSH, D], F32, kind="ExternalOutput").ap()
    dbg_scl = nc.dram_tensor("dbg_scl", [TSH, 8], F32, kind="ExternalOutput").ap() if DEBUG_CW else None
    dbg_rk = nc.dram_tensor("dbg_rk", [TSH, 2], F32, kind="ExternalOutput").ap() if DEBUG_CW else None

    p = Prog(nc)
    X = p.sb("X", [128, NT, D], F32)
    XLNROT = p.sb("XLNROT", [128, 4, D], BF16)
    XT = p.sb("XT", [128, DC, TSH], BF16)
    ident = p.sb("ident_sb", [128, 128], BF16)
    LTB = p.sb("LTB", [128, 128], BF16)
    W1s = p.sb("W1s", [128, DC, F1 // NQ], BF16)
    W2s = p.sb("W2s", [128, DC, D], BF16)
    H = p.sb("H", [128, DC, TSH // 2], BF16)
    YS = p.sb("YS", [128, 2, 512], F32)
    OHF = p.sb("OHF", [128, 128, 8], BF16)
    OHO = p.sb("OHO", [128, NT, 8], BF16)
    PM = p.sb("PM", [128, 128], BF16)
    PP = p.sb("PP", [128, NT], F32)
    CBR = p.sb("CBR", [1, 8 * (NT + 1)], F32)
    TCR = p.sb("TCR", [1, 8 * NT], F32)
    RANKC = p.sb("RANKC", [128, NT], F32)
    T8 = p.sb("T8", [128, 8], F32)
    JNK = p.sb("JNK", [128, 8], F32)
    ONECB = p.sb("ONECB", [128, 1], BF16)
    ONE32 = p.sb("ONE32", [1, 128], F32)
    TCPS = p.sb("TCPS", [16, 8], BF16)
    BASER = p.sb("BASER", [1, 8], F32)
    CBF = p.sb("CBF", [16, 8], F32)
    SCL = p.sb("SCL", [128, NT, 8], F32)
    sc = {"xln_f32": p.sb("xlnf", [128, 1, D], F32),
          "stats": p.sb("stats", [128, 16], F32)}
    kc = emit_consts(p)
    DBGRK = p.sb("DBGRK", [128, NT, 2], F32) if DEBUG_CW else None
    psh = p.ps("psh", [128, 2, 512], F32)
    psy = p.ps("psy", [128, 2, 512], F32)
    psr = p.ps("psr", [128, 2, 512], F32)

    for s_ in ("s_x", "s_c", "s_o8", "s_oh", "s_ohg", "b_mm", "b_ev", "b2_mm", "b2_ev", "s_cb", "s_cbg", "scp2", "k2_", "t_oh", "t_ohe", "rk_", "sc_",
               "scp", "s_w1", "s_w2", "h_mm", "h_ev", "y_mm", "y_sc", "y_ev", "s_y"):
        p.sem(s_)
    @p.on("sync")
    def _(e):
        e.dma_start(out=X[:, :, :], in_=x.rearrange("(t p) d -> p t d", p=128)).then_inc(p.sems["s_x"], 16)
        e.dma_start(out=PP[:, :], in_=ppin).then_inc(p.sems["s_oh"], 16)
    p.cnt["s_x"] = 16
    p.cnt["s_oh"] = 16
    @p.on("gpsimd")
    def _(e):
        e.dma_start(out=OHF[:, :, :], in_=ohf.rearrange("(p n) e -> p n e", p=128)).then_inc(p.sems["s_ohg"], 16)
        e.dma_start(out=PM[:, :], in_=pmask.rearrange("(p n) o -> p (n o)", p=128)).then_inc(p.sems["s_ohg"], 16)
        e.dma_start(out=OHO[:, :, :], in_=ohown.rearrange("(t p) e -> p t e", p=128)).then_inc(p.sems["s_ohg"], 16)
    p.cnt["s_ohg"] = 48
    @p.on("gpsimd")
    def _(e):
        e.dma_start(out=ident[:, :], in_=ident_in).then_inc(p.sems["s_c"], 16)
        e.dma_start(out=LTB[:, :], in_=lt_in).then_inc(p.sems["s_c"], 16)
        e.memset(ONECB[:, :], 1.0)
        e.memset(ONE32[:, :], 1.0)
        e.nop().then_inc(p.sems["s_o8"], 1)
    p.cnt["s_c"] = 32
    p.cnt["s_o8"] = 1

    emit_ln_transpose(p, X, XT, XLNROT, ident, sc, kc, "l",
                      add_half_to_stream=False, wait=(p.sems["s_x"], 16),
                      wait_consts=(p.sems["s_c"], 32))

    RS = 0 if skip_rank else rank_stage
    if RS < 4:
        @p.on("gpsimd")
        def _(e):
            for t in range(NT):
                e.memset(SCL[:, t, :], 0.125)
            e.nop().then_inc(p.sems["sc_"], NT)
        p.cnt["sc_"] = NT
    _, v_b = p.inc("b_mm")
    @p.on("tensor")
    def _(e):
        if RS < 1:
            e.nop().then_inc(p.sems["b_mm"], 1)
            return
        e.wait_ge(p.sems["s_oh"], 16)
        e.wait_ge(p.sems["s_ohg"], 48)
        e.wait_ge(p.sems["s_c"], 32)
        e.wait_ge(p.sems["s_o8"], 1)
        for blk in range(128):
            e.matmul(psr[0:1, 0, 0:8], PM[:, blk:blk + 1], OHF[:, blk, :],
                     start=(blk == 0), stop=(blk == 127))
            if blk % 16 == 15:
                e.drain()
        for t in range(NT):
            e.matmul(psr[0:1, 0, 8 + t * 8:16 + t * 8], ONECB[:, 0:1], OHO[:, t, :],
                     start=True, stop=True)
            if t % 4 == 3:
                e.drain()
        e.drain()
        e.nop().then_inc(p.sems["b_mm"], 1)
    _, v_be = p.inc("b_ev")
    @p.on("vector")
    def _(e, v_b=v_b):
        e.wait_ge(p.sems["b_mm"], v_b)
        if RS >= 1:
            e.tensor_copy(TCR[0:1, :], psr[0:1, 0, 8:8 + 8 * NT])
            e.drain()
            e.tensor_copy(BASER[0:1, :], psr[0:1, 0, 0:8])
        p.sig(e, "b_ev")
    @p.on("gpsimd")
    def _(e, v_be=v_be):
        e.wait_ge(p.sems["b_ev"], v_be)
        e.dma_start(out=TCPS[0:16, :], in_=TCR[0:1, 0:8 * NT]).then_inc(p.sems["s_cbg"], 16)
    p.cnt["s_cbg"] = 16
    _, v_b2 = p.inc("b2_mm")
    @p.on("tensor")
    def _(e, v_be=v_be):
        if RS < 1:
            e.nop().then_inc(p.sems["b2_mm"], 1)
            return
        e.wait_ge(p.sems["s_cbg"], 16)
        e.matmul(psr[0:16, 1, 0:8], LTB[0:16, 0:16], TCPS[0:16, :],
                 start=True, stop=True)
        e.matmul(psr[0:16, 1, 8:16], ONE32[0:1, 0:16], BASER[0:1, :],
                 start=True, stop=True)
        e.drain()
        e.nop().then_inc(p.sems["b2_mm"], 1)
    _, v_b2e = p.inc("b2_ev")
    @p.on("vector")
    def _(e, v_b2=v_b2):
        e.wait_ge(p.sems["b2_mm"], v_b2)
        if RS >= 1:
            e.tensor_copy(CBF[0:16, :], psr[0:16, 1, 8:16])
            e.drain()
            e.tensor_tensor(CBF[0:16, :], psr[0:16, 1, 0:8], CBF[0:16, :], ALU.add)
        p.sig(e, "b2_ev")
    @p.on("sync")
    def _(e, v_b2e=v_b2e):
        e.wait_ge(p.sems["b2_ev"], v_b2e)
        e.dma_start(out=CBR[0:1, 0:128], in_=CBF[0:16, :]).then_inc(p.sems["s_cb"], 16)
    p.cnt["s_cb"] = 16

    for t in (range(NT) if RS >= 2 else []):
        _, v_t = p.inc("t_oh")
        @p.on("tensor")
        def _(e, t=t):
            r = t % 2
            if t <= 1:
                e.wait_ge(p.sems["s_cb"], 16)
            if t >= 2 and RS >= 3:
                e.wait_ge(p.sems["scp"], t - 1)
            e.matmul(psr[:, r, 128:136], LTB[:, :], OHO[:, t, :],
                     start=True, stop=True)
            e.matmul(psr[:, r, 136:144], ONE32[0:1, :], CBR[0:1, t * 8:(t + 1) * 8],
                     start=True, stop=True)
            e.drain()
            e.nop().then_inc(p.sems["t_oh"], 1)
        if RS < 3:
            continue
        _, v_k = p.inc("scp")
        @p.on("vector")
        def _(e, t=t, v_t=v_t):
            r = t % 2
            e.wait_ge(p.sems["t_oh"], v_t)
            e.tensor_copy(T8[:, :], psr[:, r, 136:144])
            e.drain()
            e.tensor_tensor(T8[:, :], psr[:, r, 128:136], T8[:, :], ALU.add)
            e.drain()
            e.tensor_tensor(T8[:, :], T8[:, :], OHO[:, t, :], ALU.mult)
            e.drain()
            e.nop().then_inc(p.sems["scp"], 1)
        @p.on("scalar")
        def _(e, t=t, v_k=v_k):
            e.wait_ge(p.sems["scp"], v_k)
            e.activation(out=JNK[:, :], in_=T8[:, :], func=AF.Copy,
                         accum_out=RANKC[:, t:t + 1])
            p.sig(e, "rk_")
        if RS < 4:
            continue
        _, v_rk = p.inc("rk_")
        _, v_k1 = p.inc("scp2")
        @p.on("vector")
        def _(e, t=t, v_rk=v_rk):
            e.wait_ge(p.sems["rk_"], v_rk)
            e.tensor_scalar(sc["stats"][:, 10:11], RANKC[:, t:t + 1], CAP, None, ALU.is_lt)
            e.drain()
            e.nop().then_inc(p.sems["scp2"], 1)
        _, v_k2 = p.inc("k2_")
        @p.on("scalar")
        def _(e, t=t, v_k1=v_k1):
            e.wait_ge(p.sems["scp2"], v_k1)
            if t == 0:
                e.wait_ge(p.sems["s_oh"], 16)
            e.activation(out=sc["stats"][:, 11:12], in_=sc["stats"][:, 10:11],
                         func=AF.Copy, scale=PP[:, t:t + 1])
            p.sig(e, "k2_")
        _, v_scl = p.inc("sc_")
        @p.on("vector")
        def _(e, t=t, v_k2=v_k2):
            e.wait_ge(p.sems["k2_"], v_k2)
            e.tensor_scalar(SCL[:, t, :], OHO[:, t, :], sc["stats"][:, 11:12],
                            0.5, ALU.mult, ALU.mult)
            if DEBUG_CW:
                e.tensor_copy(DBGRK[:, t, 0:1], RANKC[:, t:t + 1])
                e.tensor_copy(DBGRK[:, t, 1:2], sc["stats"][:, 11:12])
            p.sig(e, "sc_")

    sc_total = p.cnt["sc_"]
    for ex in range(8):
        for q_ in range(NQ):
            _, v_w1 = p.inc("s_w1", 16)
            @p.on("gpsimd")
            def _(e, ex=ex, q_=q_, hneed=p.cnt["h_mm"]):
                if ex + q_ > 0:
                    e.wait_ge(p.sems["h_mm"], hneed)
                e.dma_start(out=W1s[:, :, :],
                            in_=w1[ex][:, q_ * 1024:(q_ + 1) * 1024].rearrange(
                                "(c p) h -> p c h", p=128)).then_inc(p.sems["s_w1"], 16)
            _, v_w2 = p.inc("s_w2", 16)
            @p.on("gpsimd")
            def _(e, ex=ex, q_=q_, yneed=p.cnt["y_mm"]):
                if ex + q_ > 0:
                    e.wait_ge(p.sems["y_mm"], yneed)
                e.dma_start(out=W2s[:, :, :],
                            in_=w2[ex][q_ * 1024:(q_ + 1) * 1024, :].rearrange(
                                "(c p) d -> p c d", p=128)).then_inc(p.sems["s_w2"], 16)
            for th_ in range(2):
                for hc in range(DC):
                    for tb in range(2):
                        n_h = p.cur("h_mm") + 1
                        _, v_h = p.inc("h_mm")
                        @p.on("tensor")
                        def _(e, hc=hc, th_=th_, tb=tb, n_h=n_h, w1t=p.cnt["s_w1"]):
                            r = n_h % 2
                            if hc == 0 and tb == 0 and th_ == 0:
                                e.wait_ge(p.sems["s_w1"], w1t)
                                e.wait_ge(p.sems["xt_l"], NT)
                            if n_h > 2:
                                e.wait_ge(p.sems["h_ev"], n_h - 2)
                            toff = th_ * 1024 + tb * 512
                            for c in range(DC):
                                e.matmul(psh[:, r, :],
                                         W1s[:, c, hc * 128:(hc + 1) * 128],
                                         XT[:, c, toff:toff + 512],
                                         start=(c == 0), stop=(c == DC - 1))
                            e.drain()
                            e.nop().then_inc(p.sems["h_mm"], 1)
                        _, v_he = p.inc("h_ev")
                        @p.on("scalar")
                        def _(e, hc=hc, tb=tb, n_h=n_h, v_h=v_h, yneed=p.cnt["y_mm"]):
                            r = n_h % 2
                            e.wait_ge(p.sems["h_mm"], v_h)
                            if hc == 0 and tb == 0:
                                e.wait_ge(p.sems["y_mm"], yneed)
                            e.activation(out=H[:, hc, tb * 512:(tb + 1) * 512],
                                         in_=psh[:, r, :], func=AF.Relu)
                            p.sig(e, "h_ev")
                hdone = p.cnt["h_ev"]
                for tsub in range(8):
                    ts_ = th_ * 8 + tsub
                    for dh in range(2):
                        n_y = p.cur("y_mm") + 1
                        _, v_y = p.inc("y_mm")
                        @p.on("tensor")
                        def _(e, ex=ex, q_=q_, tsub=tsub, ts_=ts_, dh=dh, n_y=n_y,
                              hdone=hdone, w2t=p.cnt["s_w2"]):
                            r = n_y % 2
                            if tsub == 0 and dh == 0:
                                e.wait_ge(p.sems["h_ev"], hdone)
                                e.wait_ge(p.sems["s_w2"], w2t)
                            if n_y > 2:
                                e.wait_ge(p.sems["y_sc"], n_y - 2)
                            for c in range(DC):
                                e.matmul(psy[:, r, :],
                                         H[:, c, tsub * 128:(tsub + 1) * 128],
                                         W2s[:, c, dh * 512:(dh + 1) * 512],
                                         start=(c == 0), stop=(c == DC - 1))
                            e.drain()
                            e.nop().then_inc(p.sems["y_mm"], 1)
                        _, v_ys = p.inc("y_sc")
                        @p.on("scalar")
                        def _(e, ex=ex, ts_=ts_, n_y=n_y, v_y=v_y, sct=sc_total):
                            r = n_y % 2
                            e.wait_ge(p.sems["y_mm"], v_y)
                            if n_y == 1:
                                e.wait_ge(p.sems["sc_"], sct)
                            if n_y > 2:
                                e.wait_ge(p.sems["y_ev"], n_y - 2)
                            e.activation(out=YS[:, r, :], in_=psy[:, r, :], func=AF.Copy,
                                         scale=SCL[:, ts_, ex:ex + 1])
                            p.sig(e, "y_sc")
                        _, v_ye = p.inc("y_ev")
                        @p.on("vector")
                        def _(e, ts_=ts_, dh=dh, n_y=n_y, v_ys=v_ys):
                            r = n_y % 2
                            e.wait_ge(p.sems["y_sc"], v_ys)
                            e.tensor_add(X[:, ts_, dh * 512:(dh + 1) * 512],
                                         X[:, ts_, dh * 512:(dh + 1) * 512], YS[:, r, :])
                            p.sig(e, "y_ev")

    if DEBUG_CW:
        @p.on("sync")
        def _(e, sct=p.cnt["sc_"]):
            e.wait_ge(p.sems["sc_"], sct)
            e.dma_start(out=dbg_scl.rearrange("(t p) e -> p t e", p=128),
                        in_=SCL[:, :, :]).then_inc(p.sems["s_y"], 16)
            e.dma_start(out=dbg_rk.rearrange("(t p) e -> p t e", p=128),
                        in_=DBGRK[:, :, :]).then_inc(p.sems["s_y"], 16)
        p.cnt["s_y"] = 32
    @p.on("sync")
    def _(e, syf=p.cnt["s_y"] + 16):
        e.wait_ge(p.sems["y_ev"], p.cnt["y_ev"])
        e.dma_start(out=y.rearrange("(t p) d -> p t d", p=128), in_=X[:, :, :]).then_inc(p.sems["s_y"], 16)
        e.wait_ge(p.sems["s_y"], syf)
    p.cnt["s_y"] += 16

    p.finish()
    return nc


def _ident_np():
    return np.eye(128, dtype=np.float32)


def _iota8_np():
    return np.tile(np.arange(8, dtype=np.float32)[None, :], (128, 1))


_NC_CACHE = {}


def _get_nc(name, builder):
    if name not in _NC_CACHE:
        _NC_CACHE[name] = builder()
    return _NC_CACHE[name]


def run_ffn0(x_full, gw, w1, b1, w2, b2):
    nc = _get_nc("ffn0", build_ffn0)
    ident = _ident_np()
    iota8 = _iota8_np()
    in_maps = []
    for c in range(NCORE):
        in_maps.append({
            "x": np.ascontiguousarray(x_full[c * TSH:(c + 1) * TSH]),
            "gw": gw, "w1": w1, "b1": b1, "w2": w2, "b2": b2,
            "ident": ident, "iota8": iota8,
        })
    res = run_bass_kernel_spmd(nc, in_maps, core_ids=list(range(NCORE)))
    return np.concatenate([res.results[c]["y"] for c in range(NCORE)], axis=0)


def build_qkvrot1():
    nc = new_nc()
    x = nc.dram_tensor("x", [TSH, D], F32, kind="ExternalInput").ap()
    wq = nc.dram_tensor("wq", [D, D], F32, kind="ExternalInput").ap()
    wkv = nc.dram_tensor("wkv", [D, 2 * D], F32, kind="ExternalInput").ap()
    qnw = nc.dram_tensor("qnw", [128, 8], F32, kind="ExternalInput").ap()
    qnb = nc.dram_tensor("qnb", [128, 8], F32, kind="ExternalInput").ap()
    kvnw = nc.dram_tensor("kvnw", [128, 8], F32, kind="ExternalInput").ap()
    kvnb = nc.dram_tensor("kvnb", [128, 8], F32, kind="ExternalInput").ap()
    qb = nc.dram_tensor("qb", [1, D], F32, kind="ExternalInput").ap()
    kvb = nc.dram_tensor("kvb", [1, 2 * D], F32, kind="ExternalInput").ap()
    lnw = nc.dram_tensor("lnw", [1, 64], F32, kind="ExternalInput").ap()
    lnb = nc.dram_tensor("lnb", [1, 64], F32, kind="ExternalInput").ap()
    rhw = nc.dram_tensor("rhw", [128, 8], F32, kind="ExternalInput").ap()
    rhb = nc.dram_tensor("rhb", [1, 1], F32, kind="ExternalInput").ap()
    cosT = nc.dram_tensor("cosT", [128, NT, 512], F32, kind="ExternalInput").ap()
    sinT = nc.dram_tensor("sinT", [128, NT, 512], F32, kind="ExternalInput").ap()
    ident_in = nc.dram_tensor("ident", [128, 128], F32, kind="ExternalInput").ap()
    dbg_rot = nc.dram_tensor("dbg_rot", [TSH, 2, D], F32, kind="ExternalOutput").ap() if DEBUG_CW else None
    dbg_rad = nc.dram_tensor("dbg_rad", [TSH, 2], F32, kind="ExternalOutput").ap() if DEBUG_CW else None
    dbg_fr = nc.dram_tensor("dbg_fr", [TSH, 512], F32, kind="ExternalOutput").ap() if DEBUG_CW else None
    qn_out = nc.dram_tensor("qn_out", [TSH, D], BF16, kind="ExternalOutput").ap()
    kn_out = nc.dram_tensor("kn_out", [TSH, D], BF16, kind="ExternalOutput").ap()
    v_out = nc.dram_tensor("v_out", [TSH, D], BF16, kind="ExternalOutput").ap()

    p = Prog(nc)
    X = p.sb("X", [128, NT, D], F32)
    XLNROT = p.sb("XLNROT", [128, 4, D], BF16)
    XT = p.sb("XT", [128, DC, TSH], BF16)
    ident = p.sb("ident_sb", [128, 128], BF16)
    WQ = p.sb("WQ", [128, DC, D], BF16)
    WKV = p.sb("WKV", [128, DC, 2 * D], BF16)
    QNWf = p.sb("QNWf", [128, 8], F32)
    KVNWf = p.sb("KVNWf", [128, 8], F32)
    QNBb = p.sb("QNBb", [128, 8], BF16)
    KVNBb = p.sb("KVNBb", [128, 8], BF16)
    QBb = p.sb("QBb", [1, D], BF16)
    KVBb = p.sb("KVBb", [1, 2 * D], BF16)
    LNWb = p.sb("LNWb", [1, 64], BF16)
    LNBb = p.sb("LNBb", [1, 64], BF16)
    RHWb = p.sb("RHWb", [128, 8], BF16)
    RHBb = p.sb("RHBb", [1, 1], BF16)
    COS = p.sb("COS", [128, 2, 512], BF16)
    SIN = p.sb("SIN", [128, 2, 512], BF16)
    BQ = p.sb("BQ", [1, D], BF16)
    BKV = p.sb("BKV", [1, 2 * D], BF16)
    WB = p.sb("WB", [128, 64], F32)
    BB = p.sb("BB", [128, 64], F32)
    RAD = p.sb("RAD", [128, 2], F32)
    FR = p.sb("FR", [128, 512], F32)
    FI = p.sb("FI", [128, 512], F32)
    TA = p.sb("TA", [128, 512], F32)
    TB = p.sb("TB", [128, 512], F32)
    ROT = p.sb("ROT", [128, 2, D], F32)
    T1 = p.sb("T1", [128, 64], F32)
    TJ = p.sb("TJ", [128, 64], F32)
    QS = p.sb("QS", [128, 2, 512], F32)
    RADU = p.sb("RADU", [128, 2], F32)
    RC = p.sb("RC", [128, 5], F32)
    OUTB = p.sb("OUTB", [128, 2, D], BF16)
    VS = p.sb("VS", [128, 2, 512], BF16)
    sc = {"xln_f32": p.sb("xlnf", [128, 1, D], F32),
          "stats": p.sb("stats", [128, 16], F32)}
    kc = emit_consts(p)
    ones = kc["ones"]
    psq = p.ps("psq", [128, 2, 512], F32)
    psv = p.ps("psv", [128, 2, 512], F32)
    psb = p.ps("psb", [128, 2, 512], F32)

    for s_ in ("s_x", "s_c", "s_ch", "s_w", "pre_mm", "pre_ev", "w_scaled", "q_mm",
               "q_ev", "q_ec", "s_dr", "rad_a", "s_rc", "rot_", "lnh_sq", "lnh_red", "lnh_a1", "lnh_d2", "lnh_ln",
               "lnh_ap", "v_mm", "v_ev", "s_st", "s_sv", "s_y"):
        p.sem(s_)
    @p.on("sync")
    def _(e):
        e.dma_start(out=X[:, :, :], in_=x.rearrange("(t p) d -> p t d", p=128)).then_inc(p.sems["s_x"], 16)
        for buf, src in ((QNWf, qnw), (KVNWf, kvnw)):
            e.dma_start(out=buf[:, :], in_=src).then_inc(p.sems["s_ch"], 16)
    p.cnt["s_x"] = 16
    p.cnt["s_ch"] = 32
    p.cnt["s_c"] = 0
    @p.on("gpsimd")
    def _(e):
        e.dma_start(out=ident[:, :], in_=ident_in).then_inc(p.sems["s_c"], 16)
        for buf, src in ((QNBb, qnb), (KVNBb, kvnb), (QBb, qb), (KVBb, kvb),
                         (LNWb, lnw), (LNBb, lnb), (RHWb, rhw), (RHBb, rhb)):
            e.dma_start(out=buf[:, :], in_=src).then_inc(p.sems["s_c"], 16)
        e.dma_start(out=WQ[:, :, :], in_=wq.rearrange("(c p) n -> p c n", p=128)).then_inc(p.sems["s_w"], 16)
        e.dma_start(out=WKV[:, :, :], in_=wkv.rearrange("(c p) n -> p c n", p=128)).then_inc(p.sems["s_w"], 16)
    p.cnt["s_c"] += 9 * 16
    @p.on("gpsimd")
    def _(e):
        for i, v in enumerate((0.01, -0.5, -1.0, 1.5, 2.0)):
            e.memset(RC[:, i:i + 1], v)
        e.nop().then_inc(p.sems["s_rc"], 1)
    p.cnt["s_rc"] = 1
    p.cnt["s_w"] = 32
    p.sem("s_cs")
    for t in range(NT):
        _, _v = p.inc("s_cs", 32)
        @p.on("gpsimd")
        def _(e, t=t):
            if t >= 2:
                e.wait_ge(p.sems["q_ev"], t - 1)
            e.dma_start(out=COS[:, t % 2, :], in_=cosT[:, t, :]).then_inc(p.sems["s_cs"], 16)
            e.dma_start(out=SIN[:, t % 2, :], in_=sinT[:, t, :]).then_inc(p.sems["s_cs"], 16)

    emit_ln_transpose(p, X, XT, XLNROT, ident, sc, kc, "l",
                      add_half_to_stream=False, wait=(p.sems["s_x"], 16),
                      wait_consts=(p.sems["s_c"], 16))

    nb = 0
    for which, BNB, W, BROW, OUT, nds in (("q", QNBb, WQ, QBb, BQ, 2),
                                          ("kv", KVNBb, WKV, KVBb, BKV, 4)):
        for dh in range(nds):
            nb += 1
            _, v_b = p.inc("pre_mm")
            @p.on("tensor")
            def _(e, BNB=BNB, W=W, BROW=BROW, dh=dh, nb=nb):
                r = nb % 2
                if nb == 1:
                    e.wait_ge(p.sems["s_c"], p.cnt["s_c"])
                    e.wait_ge(p.sems["s_ch"], 32)
                    e.wait_ge(p.sems["s_w"], 32)
                    e.wait_ge(p.sems["s_k"], 1)
                if nb > 2:
                    e.wait_ge(p.sems["pre_ev"], nb - 2)
                for c in range(DC):
                    e.matmul(psb[0:1, r, :], BNB[:, c:c + 1], W[:, c, dh * 512:(dh + 1) * 512],
                             start=(c == 0), stop=False)
                e.matmul(psb[0:1, r, :], ones[0:1, 0:1], BROW[0:1, dh * 512:(dh + 1) * 512],
                         start=False, stop=True)
                e.drain()
                e.nop().then_inc(p.sems["pre_mm"], 1)
            _, v_be = p.inc("pre_ev")
            @p.on("vector")
            def _(e, OUT=OUT, dh=dh, nb=nb, v_b=v_b):
                r = nb % 2
                e.wait_ge(p.sems["pre_mm"], v_b)
                e.tensor_copy(OUT[0:1, dh * 512:(dh + 1) * 512], psb[0:1, r, :])
                p.sig(e, "pre_ev")
    _, v_wb = p.inc("pre_mm")
    @p.on("tensor")
    def _(e, nb=nb):
        e.wait_ge(p.sems["pre_ev"], nb)
        e.matmul(psb[:, 0, 0:64], ones[0:1, 0:128], LNWb[0:1, :], start=True, stop=True)
        e.matmul(psb[:, 1, 0:64], ones[0:1, 0:128], LNBb[0:1, :], start=True, stop=True)
        e.drain()
        e.nop().then_inc(p.sems["pre_mm"], 1)
    _, v_wbe = p.inc("pre_ev")
    @p.on("vector")
    def _(e, v_wb=v_wb):
        e.wait_ge(p.sems["pre_mm"], v_wb)
        e.tensor_copy(WB[:, :], psb[:, 0, 0:64])
        e.tensor_copy(BB[:, :], psb[:, 1, 0:64])
        p.sig(e, "pre_ev")
    _, v_ws = p.inc("w_scaled")
    @p.on("vector")
    def _(e, pre_t=p.cnt["pre_ev"]):
        e.wait_ge(p.sems["pre_ev"], pre_t)
        for c in range(DC):
            e.tensor_scalar(WQ[:, c, :], WQ[:, c, :], QNWf[:, c:c + 1], None, ALU.mult)
            e.tensor_scalar(WKV[:, c, :], WKV[:, c, :], KVNWf[:, c:c + 1], None, ALU.mult)
        p.sig(e, "w_scaled")

    qn3 = qn_out.rearrange("(t p) d -> p t d", p=128)
    kn3 = kn_out.rearrange("(t p) d -> p t d", p=128)
    v3 = v_out.rearrange("(t p) d -> p t d", p=128)
    for t in range(NT):
        _, v_pm = p.inc("q_mm")
        @p.on("tensor")
        def _(e, t=t, v_ws=v_ws, qev=p.cnt["q_ev"], rott=p.cnt["rot_"]):
            if t == 0:
                e.wait_ge(p.sems["w_scaled"], v_ws)
                e.wait_ge(p.sems["xt_l"], NT)
            if t >= 1:
                e.wait_ge(p.sems["q_ev"], qev)
                e.wait_ge(p.sems["rot_"], rott)
            for c in range(DC):
                e.matmul(psq[:, 0, 0:1], XT[:, c, t * 128:(t + 1) * 128],
                         RHWb[:, c:c + 1], start=(c == 0), stop=False)
            e.matmul(psq[:, 0, 0:1], ones[0:1, 0:128], RHBb[0:1, 0:1],
                     start=False, stop=True)
            e.drain()
            e.nop().then_inc(p.sems["q_mm"], 1)
        _, v_ra = p.inc("rad_a")
        @p.on("scalar")
        def _(e, t=t, v_pm=v_pm):
            e.wait_ge(p.sems["q_mm"], v_pm)
            if t == 0:
                e.wait_ge(p.sems["s_rc"], 1)
            e.activation(out=RADU[:, 0:1], in_=psq[:, 0, 0:1], func=AF.Relu,
                         scale=RC[:, 0:1], bias=RC[:, 1:2])
            e.activation(out=RADU[:, 1:2], in_=RADU[:, 0:1], func=AF.Relu,
                         scale=RC[:, 2:3], bias=RC[:, 3:4])
            e.activation(out=RAD[:, 0:1], in_=RADU[:, 1:2], func=AF.Relu,
                         scale=RC[:, 2:3], bias=RC[:, 4:5])
            p.sig(e, "rad_a")
        _, v_rad = p.inc("q_ev")
        @p.on("vector")
        def _(e, t=t, v_ra=v_ra, drt=p.cnt["s_dr"]):
            e.wait_ge(p.sems["rad_a"], v_ra)
            if DEBUG_CW and t >= 1:
                e.wait_ge(p.sems["s_dr"], drt)
            e.wait_ge(p.sems["s_cs"], (t + 1) * 32)
            e.tensor_scalar(FR[:, :], COS[:, t % 2, :], RAD[:, 0:1], None, ALU.mult)
            e.tensor_scalar(FI[:, :], SIN[:, t % 2, :], RAD[:, 0:1], None, ALU.mult)
            p.sig(e, "q_ev")
        qsview = QS[:, :, :].rearrange("p s f -> p (s f)")
        v_rot = None
        for which, Wmat, BR, slot in (("q", WQ, BQ, 0), ("k", WKV, BKV, 1)):
            for dh in range(2):
                _, v_q = p.inc("q_mm")
                @p.on("tensor")
                def _(e, t=t, Wmat=Wmat, dh=dh, BR=BR, qev=p.cnt["q_ev"],
                      rott=p.cnt["rot_"]):
                    r = dh % 2
                    if dh == 0:
                        e.wait_ge(p.sems["q_ev"], qev)
                        e.wait_ge(p.sems["rot_"], rott)
                    for c in range(DC):
                        e.matmul(psq[:, r, :], XT[:, c, t * 128:(t + 1) * 128],
                                 Wmat[:, c, dh * 512:(dh + 1) * 512],
                                 start=(c == 0), stop=False)
                    e.matmul(psq[:, r, :], ones[0:1, 0:128], BR[0:1, dh * 512:(dh + 1) * 512],
                             start=False, stop=True)
                    e.drain()
                    e.nop().then_inc(p.sems["q_mm"], 1)
            _, v_qec = p.inc("q_ec")
            @p.on("scalar")
            def _(e, qmt=p.cnt["q_mm"]):
                e.wait_ge(p.sems["q_mm"], qmt)
                e.activation(out=QS[:, 0, :], in_=psq[:, 0, :], func=AF.Copy)
                e.activation(out=QS[:, 1, :], in_=psq[:, 1, :], func=AF.Copy)
                p.sig(e, "q_ec")
            _, v_rot = p.inc("rot_")
            @p.on("vector")
            def _(e, t=t, slot=slot, v_qec=v_qec):
                e.wait_ge(p.sems["q_ec"], v_qec)
                xr = qsview[:, 0:D:2]
                xi = qsview[:, 1:D:2]
                e.tensor_tensor(TA[:, :], xr, FR[:, :], ALU.mult)
                e.tensor_tensor(TB[:, :], xi, FI[:, :], ALU.mult)
                e.tensor_sub(ROT[:, slot, 0:D:2], TA[:, :], TB[:, :])
                e.tensor_tensor(TA[:, :], xr, FI[:, :], ALU.mult)
                e.tensor_tensor(TB[:, :], xi, FR[:, :], ALU.mult)
                e.tensor_add(ROT[:, slot, 1:D:2], TA[:, :], TB[:, :])
                p.sig(e, "rot_")
        if DEBUG_CW:
            _, v_dr = p.inc("s_dr", 48)
            @p.on("sync")
            def _(e, t=t, rt=p.cnt["rot_"], qe=p.cnt["q_ev"]):
                e.wait_ge(p.sems["rot_"], rt)
                e.wait_ge(p.sems["q_ev"], qe)
                e.dma_start(out=dbg_rot.rearrange("(t p) s d -> p t (s d)", p=128)[:, t, :],
                            in_=ROT[:, :, :].rearrange("p s d -> p (s d)")).then_inc(p.sems["s_dr"], 16)
                e.dma_start(out=dbg_rad.rearrange("(t p) o -> p t o", p=128)[:, t, :],
                            in_=RAD[:, 0:2]).then_inc(p.sems["s_dr"], 16)
                e.dma_start(out=dbg_fr.rearrange("(t p) f -> p t f", p=128)[:, t, :],
                            in_=FR[:, :]).then_inc(p.sems["s_dr"], 16)
        st = sc["stats"]
        for slot in (0, 1):
            for h in range(NHEAD):
                seg = ROT[:, slot, h * 64:(h + 1) * 64]
                _, v_s = p.inc("lnh_sq")
                @p.on("scalar")
                def _(e, seg=seg, t=t, slot=slot, h=h, v_rot=v_rot):
                    if h == 0 and slot == 0:
                        e.wait_ge(p.sems["rot_"], v_rot)
                    e.activation(out=TJ[:, :], in_=seg, func=AF.Square,
                                 accum_out=st[:, 0:1])
                    e.activation(out=TJ[:, :], in_=seg, func=AF.Copy,
                                 accum_out=st[:, 1:2]).then_inc(p.sems["lnh_sq"], 1)
                _, v_a = p.inc("lnh_a1")
                @p.on("scalar")
                def _(e, v_s=v_s):
                    e.wait_ge(p.sems["lnh_sq"], v_s)
                    e.activation(out=st[:, 2:3], in_=st[:, 1:2], func=AF.Copy,
                                 scale=kc["neginvHD"])
                    e.activation(out=st[:, 4:5], in_=st[:, 2:3],
                                 func=AF.Square).then_inc(p.sems["lnh_a1"], 1)
                _, v_d = p.inc("lnh_d2")
                @p.on("vector")
                def _(e, v_a=v_a):
                    e.wait_ge(p.sems["lnh_a1"], v_a)
                    e.scalar_tensor_tensor(st[:, 5:6], st[:, 0:1], 1.0 / HD, st[:, 4:5],
                                           ALU.mult, ALU.subtract)
                    p.sig(e, "lnh_d2")
                _, v_l = p.inc("lnh_ln")
                @p.on("scalar")
                def _(e, v_d=v_d):
                    e.wait_ge(p.sems["lnh_d2"], v_d)
                    e.activation(out=st[:, 6:7], in_=st[:, 5:6], func=AF.Ln,
                                 bias=kc["eps"])
                    e.activation(out=st[:, 7:8], in_=st[:, 6:7], func=AF.Exp,
                                 scale=kc["negh"]).then_inc(p.sems["lnh_ln"], 1)
                _, v_ap = p.inc("lnh_ap")
                @p.on("vector")
                def _(e, seg=seg, t=t, slot=slot, h=h, v_l=v_l, stt=p.cnt["s_st"]):
                    e.wait_ge(p.sems["lnh_ln"], v_l)
                    if slot == 0 and h == 0 and t >= 1:
                        e.wait_ge(p.sems["s_st"], stt)
                    e.tensor_scalar(T1[:, :], seg, st[:, 2:3], st[:, 7:8],
                                    ALU.add, ALU.mult)
                    e.tensor_tensor(T1[:, :], T1[:, :], WB[:, :], ALU.mult)
                    e.tensor_tensor(OUTB[:, slot, h * 64:(h + 1) * 64], T1[:, :],
                                    BB[:, :], ALU.add)
                    p.sig(e, "lnh_ap")
        _, v_st = p.inc("s_st", 32)
        @p.on("sync")
        def _(e, t=t, apt=p.cnt["lnh_ap"]):
            e.wait_ge(p.sems["lnh_ap"], apt)
            e.dma_start(out=qn3[:, t, :], in_=OUTB[:, 0, :]).then_inc(p.sems["s_st"], 16)
            e.dma_start(out=kn3[:, t, :], in_=OUTB[:, 1, :]).then_inc(p.sems["s_st"], 16)
        for dh in range(2):
            _, v_v = p.inc("v_mm")
            @p.on("tensor")
            def _(e, t=t, dh=dh, n=p.cnt["v_mm"], vev=p.cnt["v_ev"]):
                r = n % 2
                if n > 2:
                    e.wait_ge(p.sems["v_ev"], n - 2)
                for c in range(DC):
                    e.matmul(psv[:, r, :], XT[:, c, t * 128:(t + 1) * 128],
                             WKV[:, c, D + dh * 512:D + (dh + 1) * 512],
                             start=(c == 0), stop=False)
                e.matmul(psv[:, r, :], ones[0:1, 0:128],
                         BKV[0:1, D + dh * 512:D + (dh + 1) * 512],
                         start=False, stop=True)
                e.drain()
                e.nop().then_inc(p.sems["v_mm"], 1)
            _, v_ve = p.inc("v_ev")
            @p.on("scalar")
            def _(e, n=p.cnt["v_mm"], v_v=v_v):
                r = n % 2
                e.wait_ge(p.sems["v_mm"], v_v)
                if n > 2:
                    e.wait_ge(p.sems["s_sv"], (n - 2) * 16)
                e.activation(out=VS[:, r, :], in_=psv[:, r, :], func=AF.Copy)
                p.sig(e, "v_ev")
            _, v_sv = p.inc("s_sv", 16)
            @p.on("sync")
            def _(e, t=t, dh=dh, v_ve=v_ve, n=p.cnt["v_mm"]):
                e.wait_ge(p.sems["v_ev"], v_ve)
                e.dma_start(out=v3[:, t, dh * 512:(dh + 1) * 512],
                            in_=VS[:, n % 2, :]).then_inc(p.sems["s_sv"], 16)
    @p.on("sync")
    def _(e):
        e.wait_ge(p.sems["s_st"], p.cnt["s_st"])
        e.wait_ge(p.sems["s_sv"], p.cnt["s_sv"])
        if DEBUG_CW:
            e.wait_ge(p.sems["s_dr"], p.cnt["s_dr"])
    p.finish()
    return nc


def _iot512_np():
    return np.tile(np.arange(512, dtype=np.float32)[None, :], (128, 1))


def _costab_np():
    d2 = 32
    mel_max = 2595.0 * np.log10(1.0 + 4000.0 / 200.0)
    scale = np.power(10.0, np.linspace(0.0, mel_max, d2, dtype=np.float32) / 2595.0) - 1.0
    fb = (200.0 * scale / 1000.0).astype(np.float32)
    cts, sts = [], []
    for c in range(NCORE):
        base = (c % 4) * TSH
        pos = base + np.arange(TSH, dtype=np.float32)
        fr = pos[:, None] * fb[None, :]
        cosf = np.cos(fr).astype(np.float32)
        sinf = np.sin(fr).astype(np.float32)
        cos16 = np.tile(cosf, (1, 16))
        sin16 = np.tile(sinf, (1, 16))
        cts.append(np.ascontiguousarray(cos16.reshape(NT, 128, 512).transpose(1, 0, 2)))
        sts.append(np.ascontiguousarray(sin16.reshape(NT, 128, 512).transpose(1, 0, 2)))
    return cts, sts


def _qpos_np(core, big=False):
    if big:
        pos = np.full(TSH, 1e9, dtype=np.float32)
    else:
        pos = (core % 4) * TSH + np.arange(TSH, dtype=np.float32)
    return np.ascontiguousarray(pos.reshape(NT, 128).T)


def _col8(v):
    return np.ascontiguousarray(np.asarray(v, dtype=np.float32).reshape(8, 128).T)


def _run8(name, builder, in_maps):
    nc = _get_nc(name, builder)
    res = run_bass_kernel_spmd(nc, in_maps, core_ids=list(range(NCORE)))
    return res.results


def _ffn0_pass(x, IN):
    ident = _ident_np()
    iota8 = _iota8_np()
    im = [{"x": np.ascontiguousarray(x[c * TSH:(c + 1) * TSH]),
           "gw": IN["L0_ffn_gate_W"], "w1": IN["L0_ffn_W1"], "b1": IN["L0_ffn_b1"],
           "w2": IN["L0_ffn_W2"], "b2": IN["L0_ffn_b2"],
           "ident": ident, "iota8": iota8} for c in range(NCORE)]
    rs = _run8("ffn0", build_ffn0, im)
    return np.concatenate([rs[c]["y"] for c in range(NCORE)], axis=0)


def _attention_pass(x, qf_cores, kf_cores, v_cores, big_qpos):
    ident = _ident_np()
    iot = _iot512_np()
    kb = [np.concatenate([kf_cores[b * 4 + i] for i in range(4)], axis=1)
          for b in range(2)]
    vb = [np.concatenate([v_cores[b * 4 + i] for i in range(4)], axis=0)
          for b in range(2)]
    OF = [np.zeros((D, TSH), dtype=qf_cores[0].dtype) for _ in range(NCORE)]
    for g in range(NHEAD // ATT_NH):
        sl = slice(g * ATT_NH * 64, (g + 1) * ATT_NH * 64)
        im = []
        for c in range(NCORE):
            b = c // 4
            im.append({"qf": np.ascontiguousarray(qf_cores[c][sl]),
                       "kfull": np.ascontiguousarray(kb[b][sl]),
                       "vfull": np.ascontiguousarray(vb[b][:, sl]),
                       "qpos": _qpos_np(c, big=big_qpos),
                       "iot512": iot, "ident": ident})
        rs = _run8("attn", build_attn, im)
        for c in range(NCORE):
            OF[c][sl] = rs[c]["of"]
    return OF


def _outproj_pass(x, OF, wout, bout):
    im = [{"x": np.ascontiguousarray(x[c * TSH:(c + 1) * TSH]),
           "ofin": OF[c], "wout": wout, "bout": bout} for c in range(NCORE)]
    rs = _run8("outproj", build_outproj, im)
    return np.concatenate([rs[c]["y"] for c in range(NCORE)], axis=0)


def _ffn1_pass(x, IN):
    ident = _ident_np()
    iota8 = _iota8_np()
    im = [{"x": np.ascontiguousarray(x[c * TSH:(c + 1) * TSH]),
           "wr": IN["L1_ffn_Wr"], "ident": ident, "iota8": iota8}
          for c in range(NCORE)]
    rs = _run8("gate1", build_gate1, im)
    ohf = np.concatenate([rs[c]["oh"] for c in range(NCORE)], axis=0)
    im2 = []
    for c in range(NCORE):
        pm = np.zeros((16384, 1), dtype=np.float32)
        pm[:c * TSH] = 1.0
        im2.append({"x": np.ascontiguousarray(x[c * TSH:(c + 1) * TSH]),
                    "w1": IN["L1_ffn_W1"], "w2": IN["L1_ffn_W2"],
                    "ohf": ohf, "ohown": np.ascontiguousarray(ohf[c * TSH:(c + 1) * TSH]),
                    "ppin": rs[c]["pp"], "pmask": pm, "ident": ident,
                    "lt": np.triu(np.ones((128, 128), dtype=np.float32), 1)})
    rs2 = _run8("ffn1", build_ffn1, im2)
    return np.concatenate([rs2[c]["y"] for c in range(NCORE)], axis=0)


def _l1_attn_pass(x, IN):
    ident = _ident_np()
    cts, sts = _costab_np()
    im = []
    for c in range(NCORE):
        im.append({"x": np.ascontiguousarray(x[c * TSH:(c + 1) * TSH]),
                   "wq": IN["L1_attn_q_w"], "wkv": IN["L1_attn_kv_w"],
                   "qnw": _col8(IN["L1_attn_qn_w"]), "qnb": _col8(IN["L1_attn_qn_b"]),
                   "kvnw": _col8(IN["L1_attn_kvn_w"]), "kvnb": _col8(IN["L1_attn_kvn_b"]),
                   "qb": IN["L1_attn_q_b"][None, :], "kvb": IN["L1_attn_kv_b"][None, :],
                   "lnw": IN["L1_attn_ln_w"][None, :], "lnb": IN["L1_attn_ln_b"][None, :],
                   "rhw": _col8(IN["L1_attn_rh_w"][:, 0]),
                   "rhb": IN["L1_attn_rh_b"][None, :],
                   "cosT": cts[c], "sinT": sts[c], "ident": ident})
    rs = _run8("qkvrot1", build_qkvrot1, im)
    qf = [np.ascontiguousarray(rs[c]["qn_out"].T) for c in range(NCORE)]
    kf = [np.ascontiguousarray(rs[c]["kn_out"].T) for c in range(NCORE)]
    vv = [rs[c]["v_out"] for c in range(NCORE)]
    OF = _attention_pass(x, qf, kf, vv, big_qpos=True)
    return _outproj_pass(x, OF, IN["L1_attn_out_w"], IN["L1_attn_out_b"])


def kernel(**inputs):
    IN = {k: np.ascontiguousarray(np.asarray(v, dtype=np.float32))
          for k, v in inputs.items()}
    x = IN["L0_attn_x"].reshape(2 * T_FULL, D)
    x = _ffn0_pass(x, IN)
    ident = _ident_np()
    im = [{"x": np.ascontiguousarray(x[c * TSH:(c + 1) * TSH]),
           "wqkv": IN["L0_attn_w_qkv"], "ident": ident} for c in range(NCORE)]
    rs = _run8("qkv0", build_qkv0, im)
    qf = [rs[c]["qf"] for c in range(NCORE)]
    kf = [rs[c]["kf"] for c in range(NCORE)]
    vv = [rs[c]["vt"] for c in range(NCORE)]
    OF = _attention_pass(x, qf, kf, vv, big_qpos=False)
    x = _outproj_pass(x, OF, IN["L0_attn_w_out"], np.zeros(D, dtype=np.float32))
    x = _ffn0_pass(x, IN)
    x = _ffn1_pass(x, IN)
    x = _l1_attn_pass(x, IN)
    x = _ffn1_pass(x, IN)
    return x.reshape(2, T_FULL, D)
